# version 01
# baseline (speedup 1.0000x reference)
import numpy as np
from contextlib import ExitStack
import concourse.bass as bass
import concourse.mybir as mybir
from concourse.bass_utils import run_bass_kernel_spmd
from concourse.bass_core_rust import IndirectOffsetOnAxis

F32 = mybir.dt.float32
BF16 = mybir.dt.bfloat16
I32 = mybir.dt.int32
ALU = mybir.AluOpType
AF = mybir.ActivationFunctionType
AX = mybir.AxisListType

D = 1024
S = 4096
DCH = D // 128
TB = 512
NTB = S // TB
ENGINES = ("sync", "vector", "scalar", "tensor", "gpsimd")


class Tok:
    __slots__ = ("sem", "val", "eng", "comp")

    def __init__(self, sem, val, eng, comp):
        self.sem, self.val, self.eng, self.comp = sem, val, eng, comp


NDMASEM = {"sync": 8, "gpsimd": 6, "vector": 2, "scalar": 2, "tensor": 2}


class Sched:

    def __init__(self, nc):
        self.nc = nc
        self.q = {e: [] for e in ENGINES}
        self.ccnt = {e: 0 for e in ENGINES}
        self.didx = {e: 0 for e in ENGINES}
        self.dsem_cnt = {}
        self.pending_gate = {e: [] for e in ENGINES}
        self.semnames = set()
        for e in ENGINES:
            self.semnames.add(f"c_{e}")
            for j in range(NDMASEM[e]):
                self.semnames.add(f"d_{e}_{j}")

    def _dep_list(self, deps):
        out = []
        for d in deps:
            if d is None:
                continue
            if isinstance(d, Tok):
                out.append(d)
            else:
                out.extend(x for x in d if x is not None)
        return out

    def barrier(self):
        toks = []
        for e in ENGINES:
            if self.ccnt[e]:
                toks.append(Tok(f"c_{e}", self.ccnt[e], e, True))
        for s, v in self.dsem_cnt.items():
            toks.append(Tok(s, v, None, False))
        return toks

    def gate(self, deps=None):
        toks = self._dep_list(deps) if deps is not None else self.barrier()
        for e in ENGINES:
            self.pending_gate[e] = list(toks)
        return toks

    def op(self, eng, fn, deps=()):
        deps = self._dep_list(deps)
        if self.pending_gate[eng]:
            deps = self.pending_gate[eng] + deps
            self.pending_gate[eng] = []
        self.ccnt[eng] += 1
        tok = Tok(f"c_{eng}", self.ccnt[eng], eng, True)
        self.q[eng].append((fn, deps, None, tok))
        return tok

    def opd(self, eng, fn, deps=()):
        deps = self._dep_list(deps)
        if self.pending_gate[eng]:
            deps = self.pending_gate[eng] + deps
            self.pending_gate[eng] = []
        j = self.didx[eng] % NDMASEM[eng]
        self.didx[eng] += 1
        sem = f"d_{eng}_{j}"
        prev = self.dsem_cnt.get(sem, 0)
        if prev:
            deps = deps + [Tok(sem, prev, None, False)]
        self.dsem_cnt[sem] = prev + 16
        tok = Tok(sem, prev + 16, eng, False)
        self.q[eng].append((fn, deps, sem, tok))
        return tok

    def dma(self, eng, out, in_, deps=(), **kw):
        return self.opd(eng, (lambda h, out=out, in_=in_, kw=kw:
                              h.dma_start(out=out, in_=in_, **kw)), deps)

    def opcc(self, eng, fn, deps=()):
        deps = self._dep_list(deps)
        if self.pending_gate[eng]:
            deps = self.pending_gate[eng] + deps
            self.pending_gate[eng] = []
        self.cc_cnt = getattr(self, "cc_cnt", 0) + 1
        self.semnames.add("cc_sem")
        tok = Tok("cc_sem", self.cc_cnt, eng, False)
        self.q[eng].append((fn, deps, "cc", tok))
        return tok

    def emit(self):
        nc = self.nc
        with ExitStack() as ctx:
            sems = {}
            for s in sorted(self.semnames):
                sems[s] = ctx.enter_context(nc.semaphore(s))
            block = ctx.enter_context(nc.Block())

            def make_body(eng):
                def body(h):
                    waited = {}
                    for fn, deps, dsem, tok in self.q[eng]:
                        for dep in deps:
                            if (dep.comp and dep.eng == eng == "tensor"):
                                continue
                            if waited.get(dep.sem, 0) < dep.val:
                                h.wait_ge(sems[dep.sem], dep.val)
                                waited[dep.sem] = dep.val
                        if dsem == "cc":
                            fn(h).then_inc(sems[tok.sem])
                        else:
                            fn(h).then_inc(sems[tok.sem], 16 if dsem else 1)
                return body

            for e in ENGINES:
                if self.q[e]:
                    getattr(block, e)(make_body(e))


def lp_op(S_, eng, fn, deps=()):
    nc = S_.nc
    def wrapped(h):
        with nc.allow_low_precision(reason="exact small ints in bf16"):
            return fn(h)
    return S_.op(eng, wrapped, deps)



def ceil_div(a, b):
    return -(-a // b)


def bf16_np(x):
    import ml_dtypes
    return np.ascontiguousarray(x).astype(ml_dtypes.bfloat16)


class Ctx:

    def __init__(self, nc, S_):
        self.nc = nc
        self.S = S_
        self._scopes = [ExitStack()]
        self._n = 0

    def sb(self, name, shape, dtype):
        return self._scopes[0].enter_context(self.nc.sbuf_tensor(name, shape, dtype))

    def ps(self, name, shape, dtype=F32):
        return self._scopes[0].enter_context(self.nc.psum_tensor(name, shape, dtype))

    def push(self):
        self._scopes.append(ExitStack())
        self.S.gate()

    def psb(self, name, shape, dtype):
        self._n += 1
        return self._scopes[-1].enter_context(
            self.nc.sbuf_tensor(f"{name}_{self._n}", shape, dtype))

    def pps(self, name, shape, dtype=F32):
        self._n += 1
        return self._scopes[-1].enter_context(
            self.nc.psum_tensor(f"{name}_{self._n}", shape, dtype))

    def pop(self):
        self._scopes.pop().close()
        self.S.gate()

    phase_open = push
    phase_close = pop


def make_host_consts():
    import ml_dtypes
    c = {}
    c["c_ones_col_bf"] = np.ones((128, 1), ml_dtypes.bfloat16)
    c["c_ones_row_f"] = np.ones((1, 128), np.float32)
    c["c_ones_row_bf"] = np.ones((1, 128), ml_dtypes.bfloat16)
    c["c_ident_bf"] = np.eye(128, dtype=ml_dtypes.bfloat16)
    c["c_ident_f"] = np.eye(128, dtype=np.float32)
    c["c_ustrict_bf"] = np.triu(np.ones((128, 128)), 1).astype(ml_dtypes.bfloat16)
    _mi = (np.arange(32)[None, :] * 128
           + np.arange(128)[:, None]).astype(np.int32)
    c["c_moe_iota"] = np.ascontiguousarray(np.repeat(_mi, 8, axis=1))
    c["c_zero_col"] = np.zeros((128, 1), np.float32)
    return c


def load_consts(C, cst_dram):
    S_ = C.S
    cst = {}
    toks = []
    for name, ap in cst_dram.items():
        t = C.sb("sb_" + name, list(ap.shape), ap.dtype)
        toks.append(S_.dma("sync", t[:], ap[:]))
        cst[name[2:]] = t
    cst["_ready"] = toks
    return cst


def emit_ln(C, cst, x_dram, xb, xn_dram=None):
    nc, S_ = C.nc, C.S
    C.push()
    xf = [C.psb("ln_xf", [128, S], F32) for _ in range(2)]
    sq = C.psb("ln_sq", [128, TB], BF16)
    srow = C.psb("ln_srow", [1, 5 * TB], F32)
    m_sl = srow[:, 0:TB]
    q_sl = srow[:, TB:2 * TB]
    msq_sl = srow[:, 2 * TB:3 * TB]
    r_sl = srow[:, 3 * TB:4 * TB]
    n_sl = srow[:, 4 * TB:5 * TB]
    rbc = C.psb("ln_rbc", [128, S], F32)
    nrmbc = C.psb("ln_nrmbc", [128, S], F32)
    ps_s = C.pps("ln_ps_s", [1, TB])
    ps_q = C.pps("ln_ps_q", [1, TB])
    ps_b = C.pps("ln_ps_b", [128, TB])

    ld = [None] * DCH
    cast = [None] * DCH
    for c in range(DCH):
        buf = xf[c % 2]
        war = cast[c - 2] if c >= 2 else None
        ld[c] = S_.dma("sync", buf[:], x_dram[c * 128:(c + 1) * 128, :],
                       deps=[war])
        cast[c] = S_.op("vector",
                        lambda h, b=buf, o=xb[c]: h.tensor_copy(o[:], b[:]),
                        deps=[ld[c]])

    bcast_done = []
    prev_evac = None
    prev_b = None
    for t in range(NTB):
        sl = slice(t * TB, (t + 1) * TB)
        mm_s = mm_q = None
        for c in range(DCH):
            sqop = S_.op("scalar",
                         lambda h, c=c, sl=sl:
                         h.activation(sq[:], xb[c][:, sl], AF.Square),
                         deps=[cast[c], mm_q])
            mm_s = S_.op("tensor",
                         lambda h, c=c, sl=sl, st=(c == 0), sp=(c == DCH - 1):
                         h.matmul(ps_s[:], cst["ones_col_bf"][:], xb[c][:, sl],
                                  start=st, stop=sp),
                         deps=[cast[c], cst["_ready"], prev_evac])
            mm_q = S_.op("tensor",
                         lambda h, st=(c == 0), sp=(c == DCH - 1):
                         h.matmul(ps_q[:], cst["ones_col_bf"][:], sq[:],
                                  start=st, stop=sp),
                         deps=[sqop, prev_evac])
        e1 = S_.op("vector", lambda h: h.tensor_copy(m_sl, ps_s[:]),
                   deps=[mm_s, prev_b])
        e2 = S_.op("vector", lambda h: h.tensor_copy(q_sl, ps_q[:]),
                   deps=[mm_q, prev_b])
        o1 = S_.op("vector", lambda h: h.tensor_scalar_mul(m_sl, m_sl, 1.0 / D),
                   deps=[e1])
        o2 = S_.op("scalar", lambda h: h.activation(msq_sl, m_sl, AF.Square),
                   deps=[o1])
        o3 = S_.op("vector", lambda h: h.scalar_tensor_tensor(
            q_sl, q_sl, 1.0 / D, msq_sl, ALU.mult, ALU.subtract),
            deps=[e2, o2])
        o3b = S_.op("vector", lambda h: h.tensor_scalar_add(q_sl, q_sl, 1e-5),
                    deps=[o3])
        o4 = S_.op("scalar", lambda h: h.activation(r_sl, q_sl, AF.Sqrt),
                   deps=[o3b])
        o5 = S_.op("vector", lambda h: h.reciprocal(r_sl, r_sl), deps=[o4])
        o6 = S_.op("vector", lambda h: h.scalar_tensor_tensor(
            n_sl, r_sl, -1.0, m_sl, ALU.mult, ALU.mult), deps=[o5])
        prev_evac = o6
        m1 = S_.op("tensor", lambda h: h.matmul(
            ps_b[:], cst["ones_row_f"][:], r_sl, start=True, stop=True),
            deps=[o5, prev_b])
        b1 = S_.op("vector", lambda h, sl=sl: h.tensor_copy(rbc[:, sl], ps_b[:]),
                   deps=[m1])
        m2 = S_.op("tensor", lambda h: h.matmul(
            ps_b[:], cst["ones_row_f"][:], n_sl, start=True, stop=True),
            deps=[o6, b1])
        b2 = S_.op("vector", lambda h, sl=sl:
                   h.tensor_copy(nrmbc[:, sl], ps_b[:]), deps=[m2])
        prev_b = b2
        bcast_done.append(b2)

    xn_toks = []
    for c in range(DCH):
        t1 = S_.op("vector", lambda h, c=c: h.tensor_tensor(
            xb[c][:], xb[c][:], rbc[:], ALU.mult), deps=bcast_done + [cast[c]])
        t2 = S_.op("vector", lambda h, c=c: h.tensor_tensor(
            xb[c][:], xb[c][:], nrmbc[:], ALU.add), deps=[t1])
        xn_toks.append(t2)
        if xn_dram is not None:
            S_.dma("gpsimd", xn_dram[c * 128:(c + 1) * 128, :], xb[c][:],
                   deps=[t2])
    C.pop()
    return xn_toks


def emit_proj(C, cst, xn_sb, xn_toks, W_dram, E, out_dram, bias_dram=None,
              act=AF.Identity, scale=1.0, wdtype=F32, wscale=1.0,
              out_sb=None):
    nc, S_ = C.nc, C.S
    Din = W_dram.shape[0]
    KCH = Din // 128
    ESUP = 512
    nsup = (E + ESUP - 1) // ESUP
    wstg = [C.psb(f"pj_wstg{i}_{W_dram.name}", [128, ESUP], wdtype)
            for i in range(2)]
    wb = [[C.psb(f"pj_wb{i}k{k}_{W_dram.name}", [128, ESUP], BF16)
           for k in range(KCH)] for i in range(2)]
    ob = [C.psb(f"pj_ob{i}_{W_dram.name}", [128, TB], BF16) for i in range(2)]
    psum = [C.pps(f"pj_ps{i}_{W_dram.name}", [128, TB]) for i in range(2)]
    if bias_dram is not None:
        ECH = E // 128
        bias_sb = C.psb(f"pj_bias_{W_dram.name}", [128, ECH], F32)
        brd = S_.dma("sync", bias_sb[:],
                     bias_dram.rearrange("(c p) -> p c", p=128))
    evac_prev = [None, None]
    cast_prev = {}
    out_toks = []
    for sup in range(nsup):
        ecols = min(ESUP, E - sup * ESUP)
        nech = ecols // 128
        ibuf = sup % 2
        casts = []
        for k in range(KCH):
            war = cast_prev.get((sup - 2, k))
            ldw = S_.dma("sync", wstg[ibuf][:, :ecols],
                         W_dram[k * 128:(k + 1) * 128,
                                sup * ESUP:sup * ESUP + ecols],
                         deps=[war, cast_prev.get((sup - 1, k))])
            if wdtype == I32:
                cst_op = S_.op("scalar",
                               lambda h, k=k, i=ibuf, ec=ecols:
                               h.activation(wb[i][k][:, :ec], wstg[i][:, :ec],
                                            AF.Identity, scale=wscale),
                               deps=[ldw])
            else:
                cst_op = S_.op("vector",
                               lambda h, k=k, i=ibuf, ec=ecols:
                               h.tensor_copy(wb[i][k][:, :ec], wstg[i][:, :ec]),
                               deps=[ldw])
            cast_prev[(sup, k)] = cst_op
            casts.append(cst_op)
        for ec in range(nech):
            eabs = sup * ESUP + ec * 128
            for t in range(NTB):
                pb = (ec * NTB + t) % 2
                sl = slice(t * TB, (t + 1) * TB)
                mm = None
                for k in range(KCH):
                    mm = S_.op("tensor", lambda h, k=k, i=ibuf, ec=ec, sl=sl,
                               pb=pb, st=(k == 0), sp=(k == KCH - 1):
                               h.matmul(psum[pb][:], wb[i][k][:, ec * 128:(ec + 1) * 128],
                                        xn_sb[k][:, sl], start=st, stop=sp),
                               deps=[casts[k], xn_toks[k] if xn_toks else None,
                                     evac_prev[pb]])
                bias_ap = None
                if bias_dram is not None:
                    bcol = bias_sb[:, eabs // 128:eabs // 128 + 1]
                ev = S_.op("scalar", lambda h, pb=pb, b=(bias_sb[:, eabs // 128:eabs // 128 + 1] if bias_dram is not None else 0.0):
                           h.activation(ob[pb][:], psum[pb][:], act, bias=b,
                                        scale=scale),
                           deps=[mm] + ([brd] if bias_dram is not None else []))
                evac_prev[pb] = ev
                if out_dram is not None:
                    wr = S_.dma("gpsimd", out_dram[eabs:eabs + 128, sl],
                                ob[pb][:], deps=[ev])
                    out_toks.append(wr)
                if out_sb is not None:
                    cp = S_.op("vector", lambda h, pb=pb, eabs=eabs, sl=sl:
                               h.tensor_copy(out_sb[eabs // 128][:, sl], ob[pb][:]),
                               deps=[ev])
                    evac_prev[pb] = cp
                    out_toks.append(cp)
    return out_toks


def emit_proj_rhs(C, cst, rhs_dram, W_dram, E, out_dram, bias_sb=None,
                  act=AF.Identity, scale=1.0, wdtype=F32, wscale=1.0):
    nc, S_ = C.nc, C.S
    K = W_dram.shape[0]
    KCH = K // 128
    ECH = E // 128
    nm = W_dram.name
    wstg = C.psb(f"pr_wstg_{nm}", [128, E], wdtype)
    wsb = [C.psb(f"pr_w{k}_{nm}", [128, E], BF16) for k in range(KCH)]
    rsb = [[C.psb(f"pr_r{i}k{k}_{nm}", [128, TB], BF16) for k in range(KCH)]
           for i in range(2)]
    ob = [C.psb(f"pr_ob{i}_{nm}", [128, TB], BF16) for i in range(2)]
    psum = [C.pps(f"pr_ps{i}_{nm}", [128, TB]) for i in range(2)]
    casts = []
    prev_cast = None
    for k in range(KCH):
        ldw = S_.dma("sync", wstg[:], W_dram[k * 128:(k + 1) * 128, :],
                     deps=[prev_cast])
        eng = "scalar" if wdtype == I32 else "vector"
        fn = ((lambda h, k=k: h.activation(wsb[k][:], wstg[:], AF.Identity,
                                           scale=wscale))
              if wdtype == I32 else
              (lambda h, k=k: h.tensor_copy(wsb[k][:], wstg[:])))
        prev_cast = S_.op(eng, fn, deps=[ldw])
        casts.append(prev_cast)
    evac_prev = [None, None]
    ob_free = [None, None]
    ld_prev = {}
    out_toks = []
    for t in range(NTB):
        ib = t % 2
        sl = slice(t * TB, (t + 1) * TB)
        lds = []
        for k in range(KCH):
            war = ld_prev.get((t - 2, k))
            lds.append(S_.dma("sync", rsb[ib][k][:],
                              rhs_dram[k * 128:(k + 1) * 128, sl], deps=[war]))
        for ec in range(ECH):
            pb = (t * ECH + ec) % 2
            mm = None
            for k in range(KCH):
                mm = S_.op("tensor", lambda h, k=k, ib=ib, ec=ec, pb=pb,
                           st=(k == 0), sp=(k == KCH - 1):
                           h.matmul(psum[pb][:], wsb[k][:, ec * 128:(ec + 1) * 128],
                                    rsb[ib][k][:], start=st, stop=sp),
                           deps=[casts[k], lds[k], evac_prev[pb]])
            for k in range(KCH):
                ld_prev[(t, k)] = mm
            ev = S_.op("scalar", lambda h, pb=pb,
                       b=(bias_sb[:, ec:ec + 1] if bias_sb is not None else 0.0):
                       h.activation(ob[pb][:], psum[pb][:], act, bias=b,
                                    scale=scale),
                       deps=[mm, ob_free[pb]])
            evac_prev[pb] = ev
            wr = S_.dma("gpsimd", out_dram[ec * 128:(ec + 1) * 128, sl],
                        ob[pb][:], deps=[ev])
            ob_free[pb] = wr
            out_toks.append(wr)
    return out_toks


def emit_vtok(C, cst, xn_sb, xn_toks, Wv_dram, bv_dram, v_tok_dram):
    nc, S_ = C.nc, C.S
    E = Wv_dram.shape[1]
    wstg = C.psb("vt_wstg", [128, E], F32)
    wsb = [C.psb(f"vt_w{k}", [128, E], BF16) for k in range(DCH)]
    bvf = C.psb("vt_bvf", [1, E], F32) if bv_dram is not None else None
    bvb = C.psb("vt_bvb", [1, E], BF16) if bv_dram is not None else None
    ob = [C.psb(f"vt_ob{i}", [128, TB], BF16) for i in range(2)]
    psum = [C.pps(f"vt_ps{i}", [128, TB]) for i in range(2)]
    prev = None
    casts = []
    for k in range(DCH):
        ldw = S_.dma("sync", wstg[:], Wv_dram[k * 128:(k + 1) * 128, :],
                     deps=[prev])
        prev = S_.op("vector", lambda h, k=k: h.tensor_copy(wsb[k][:], wstg[:]),
                     deps=[ldw])
        casts.append(prev)
    if bv_dram is not None:
        ldb = S_.dma("sync", bvf[:], bv_dram[:])
        cb = S_.op("vector", lambda h: h.tensor_copy(bvb[:], bvf[:]), deps=[ldb])
    else:
        cb = None
    evac_prev = [None, None]
    ob_free = [None, None]
    toks = []
    for tc in range(S // 128):
        for eh in range(E // TB):
            pb = (tc * (E // TB) + eh) % 2
            esl = slice(eh * TB, (eh + 1) * TB)
            mm = None
            for k in range(DCH):
                mm = S_.op("tensor", lambda h, k=k, tc=tc, esl=esl, pb=pb,
                           st=(k == 0),
                           sp=(k == DCH - 1 and bv_dram is None):
                           h.matmul(psum[pb][:], xn_sb[k][:, tc * 128:(tc + 1) * 128],
                                    wsb[k][:, esl], start=st, stop=sp),
                           deps=[casts[k], xn_toks[k] if xn_toks else None,
                                 evac_prev[pb]])
            if bv_dram is not None:
                mm = S_.op("tensor", lambda h, esl=esl, pb=pb:
                           h.matmul(psum[pb][:], cst["ones_row_bf"][:],
                                    bvb[:, esl], start=False, stop=True),
                           deps=[mm, cb])
            ev = S_.op("scalar", lambda h, pb=pb:
                       h.activation(ob[pb][:], psum[pb][:], AF.Identity),
                       deps=[mm, ob_free[pb]])
            evac_prev[pb] = ev
            wr = S_.dma("gpsimd", v_tok_dram[tc * 128:(tc + 1) * 128, esl],
                        ob[pb][:], deps=[ev])
            ob_free[pb] = wr
            toks.append(wr)
    return toks


def emit_attn_l2(C, cst, q_dram, k_dram, v_tok_dram, o_dram):
    nc, S_ = C.nc, C.S
    negm = C.sb("a2p_negm", [1, NTB], F32)
    negmb = C.sb("a2p_negmb", [128, NTB], F32)
    C.phase_open()
    xs = [C.psb(f"a2p_x{i}", [128, S], BF16) for i in range(2)]
    sqt = C.psb("a2p_sq", [128, S], BF16)
    nrow = C.psb("a2p_nrow", [1, 2 * S], F32)
    qn2, kn2 = nrow[:, 0:S], nrow[:, S:2 * S]
    kmax = C.psb("a2p_kmax", [1, 1], F32)
    bound = C.psb("a2p_bound", [1, S], F32)
    psn = C.pps("a2p_psn", [1, S])
    for which, (src, row) in enumerate(((q_dram, qn2), (k_dram, kn2))):
        mm_done = []
        for c in range(DCH):
            buf = xs[c % 2]
            war = mm_done[c - 2] if c >= 2 else None
            ld = S_.dma("sync", buf[:], src[c * 128:(c + 1) * 128, :],
                        deps=[war])
            sq = S_.op("scalar", lambda h, b=buf: h.activation(
                sqt[:], b[:], AF.Square), deps=[ld, mm_done[-1] if c else None])
            mm = None
            for t in range(NTB):
                mm = S_.op("tensor", lambda h, t=t, st=(c == 0),
                           sp=(c == DCH - 1):
                           h.matmul(psn[:, t * TB:(t + 1) * TB],
                                    cst["ones_col_bf"][:],
                                    sqt[:, t * TB:(t + 1) * TB],
                                    start=st, stop=sp),
                           deps=[sq, qdone if (which == 1 and c == 0) else None])
            mm_done.append(mm)
        ev = S_.op("vector", lambda h, row=row: h.tensor_copy(row[:], psn[:]),
                   deps=[mm_done[-1]])
        if which == 0:
            qdone = ev
        else:
            kdone = ev
    r1 = S_.op("scalar", lambda h: h.activation(qn2[:], qn2[:], AF.Sqrt),
               deps=[qdone])
    r2 = S_.op("scalar", lambda h: h.activation(kn2[:], kn2[:], AF.Sqrt),
               deps=[kdone, r1])
    r3 = S_.op("vector", lambda h: h.reduce_max(kmax[:], kn2[:], axis=AX.X),
               deps=[r2])
    r4 = S_.op("gpsimd", lambda h: h.tensor_scalar(
        bound[:], qn2[:], kmax[:, 0:1], 1.0 / 32.0, ALU.mult, ALU.mult),
        deps=[r3, r1])
    r5 = S_.op("vector", lambda h: h.tensor_reduce(
        negm[:].unsqueeze(2), bound.rearrange("a (b c) -> a b c", c=TB),
        AX.X, ALU.max), deps=[r4])
    r5b = S_.op("gpsimd", lambda h: h.tensor_scalar_mul(negm[:], negm[:], -1.0),
                deps=[r5])
    C.phase_close()
    C.phase_open()
    psb2 = C.pps("a2p_psb", [128, NTB])
    m1 = S_.op("tensor", lambda h: h.matmul(
        psb2[:], cst["ones_row_f"][:], negm[:], start=True, stop=True),
        deps=[r5b])
    r6 = S_.op("vector", lambda h: h.tensor_copy(negmb[:], psb2[:]), deps=[m1])
    C.phase_close()

    NP = 2
    C.phase_open()
    qsb = [C.psb(f"a2_q{c}", [128, NP * TB], BF16) for c in range(DCH)]
    ksb = [C.psb(f"a2_k{i}", [128, 128], BF16) for i in range(2 * DCH)]
    vsb = [C.psb(f"a2_v{u}", [128, D], BF16) for u in range(S // 128)]
    pt = [C.psb(f"a2_pt{u}", [128, NP * TB], BF16) for u in range(S // 128)]
    lrow = C.psb("a2_lrow", [1, NP * TB], F32)
    rlb = C.psb("a2_rlb", [128, NP * TB], F32)
    obuf = [C.psb(f"a2_ob{i}", [128, TB], BF16) for i in range(2)]
    ps_s = [C.pps(f"a2_pss{i}", [128, TB]) for i in range(2)]
    ps_l = [C.pps(f"a2_psl{i}", [1, TB]) for i in range(NP)]
    ps_o = [C.pps(f"a2_pso{i}", [128, TB]) for i in range(2)]
    ps_b = C.pps("a2_psb", [128, TB])
    for pair in range(NTB // NP):
        S_.gate()
        tsl0 = pair * NP * TB

        qld = [S_.dma("sync", qsb[c][:], q_dram[c * 128:(c + 1) * 128,
                                                tsl0:tsl0 + NP * TB])
               for c in range(DCH)]
        vld = [S_.dma("gpsimd", vsb[u][:],
                      v_tok_dram[u * 128:(u + 1) * 128, :])
               for u in range(S // 128)]
        exp_prev = [None, None]
        lmm_prev = [None] * NP
        kld_war = {}
        pt_done = []
        for u in range(S // 128):
            ib = u % 2
            klds = []
            for c in range(DCH):
                war = kld_war.get((u - 2, c))
                klds.append(S_.dma("sync", ksb[ib * DCH + c][:],
                                   k_dram[c * 128:(c + 1) * 128,
                                          u * 128:(u + 1) * 128], deps=[war]))
            for j in range(NP):
                pb = (u * NP + j) % 2
                mm = None
                for c in range(DCH):
                    mm = S_.op("tensor", lambda h, c=c, j=j, ib=ib, pb=pb,
                               st=(c == 0), sp=(c == DCH - 1):
                               h.matmul(ps_s[pb][:], ksb[ib * DCH + c][:],
                                        qsb[c][:, j * TB:(j + 1) * TB],
                                        start=st, stop=sp),
                               deps=[klds[c], qld[c], exp_prev[pb]])
                    kld_war[(u, c)] = mm
                ex = S_.op("scalar", lambda h, j=j, pb=pb, u=u:
                           h.activation(pt[u][:, j * TB:(j + 1) * TB],
                                        ps_s[pb][:], AF.Exp, scale=1.0 / 32.0,
                                        bias=negmb[:, pair * NP + j:pair * NP + j + 1]),
                           deps=[mm, r6])
                exp_prev[pb] = ex
                lmm_prev[j] = S_.op("tensor", lambda h, j=j, u=u,
                                    st=(u == 0), sp=(u == S // 128 - 1):
                                    h.matmul(ps_l[j][:], cst["ones_col_bf"][:],
                                             pt[u][:, j * TB:(j + 1) * TB],
                                             start=st, stop=sp),
                                    deps=[ex, lmm_prev[j]])
                if j == NP - 1:
                    pt_done.append(ex)
        lev = S_.op("vector", lambda h: h.tensor_copy(
            lrow[:, 0:TB], ps_l[0][:]), deps=[lmm_prev[0]])
        lev2 = S_.op("vector", lambda h: h.tensor_copy(
            lrow[:, TB:2 * TB], ps_l[1][:]), deps=[lmm_prev[1], lev])
        rcp = S_.op("vector", lambda h: h.reciprocal(lrow[:], lrow[:]),
                    deps=[lev2])
        rb_toks = []
        prevb = None
        for j in range(NP):
            mb = S_.op("tensor", lambda h, j=j: h.matmul(
                ps_b[:], cst["ones_row_f"][:], lrow[:, j * TB:(j + 1) * TB],
                start=True, stop=True), deps=[rcp, prevb])
            eb = S_.op("vector", lambda h, j=j: h.tensor_copy(
                rlb[:, j * TB:(j + 1) * TB], ps_b[:]), deps=[mb])
            prevb = eb
            rb_toks.append(eb)
        evac_prev = [None, None]
        ob_free = [None, None]
        for e in range(DCH):
            for j in range(NP):
                pb = (e * NP + j) % 2
                mm = None
                for u in range(S // 128):
                    mm = S_.op("tensor", lambda h, u=u, e=e, j=j, pb=pb,
                               st=(u == 0), sp=(u == S // 128 - 1):
                               h.matmul(ps_o[pb][:], vsb[u][:, e * 128:(e + 1) * 128],
                                        pt[u][:, j * TB:(j + 1) * TB],
                                        start=st, stop=sp),
                               deps=[vld[u], pt_done[u], evac_prev[pb]])
                ev = S_.op("vector", lambda h, pb=pb, j=j: h.tensor_tensor(
                    obuf[pb][:], ps_o[pb][:], rlb[:, j * TB:(j + 1) * TB],
                    ALU.mult), deps=[mm, rb_toks[j], ob_free[pb]])
                evac_prev[pb] = ev
                wr = S_.dma("gpsimd", o_dram[e * 128:(e + 1) * 128,
                                             tsl0 + j * TB:tsl0 + (j + 1) * TB],
                            obuf[pb][:], deps=[ev])
                ob_free[pb] = wr
    C.phase_close()


def emit_glu(C, cst, xn_sb, xn_toks, Wg_dram, Wu_dram, E, h_dram,
             wdtype=F32, wscale=1.0):
    nc, S_ = C.nc, C.S
    ESUP = 512
    nsup = E // ESUP
    wstg = [C.psb(f"gl_wstg{i}", [128, ESUP], wdtype) for i in range(2)]
    wgb = [[C.psb(f"gl_wg{i}k{k}", [128, ESUP], BF16) for k in range(DCH)]
           for i in range(2)]
    wub = [[C.psb(f"gl_wu{i}k{k}", [128, ESUP], BF16) for k in range(DCH)]
           for i in range(2)]
    gb = [C.psb(f"gl_gb{i}", [128, TB], BF16) for i in range(2)]
    hb = [C.psb(f"gl_hb{i}", [128, TB], BF16) for i in range(2)]
    ps_g = [C.pps(f"gl_psg{i}", [128, TB]) for i in range(2)]
    ps_u = [C.pps(f"gl_psu{i}", [128, TB]) for i in range(2)]
    cast_prev = {}
    mm_tail = {}
    toks = []
    sil_prev = [None, None]
    hb_free = [None, None]
    for sup in range(nsup):
        ib = sup % 2
        casts = []
        for gu, wdst in ((0, wgb), (1, wub)):
            Wd = Wg_dram if gu == 0 else Wu_dram
            for k in range(DCH):
                war = cast_prev.get((sup - 2, gu, k))
                stg_war = (cast_prev.get((sup, gu, k - 1)) if k else
                           (cast_prev.get((sup, 0, DCH - 1)) if gu else
                            cast_prev.get((sup - 2, 1, DCH - 1))))
                ldw = S_.dma("sync", wstg[ib][:],
                             Wd[k * 128:(k + 1) * 128,
                                sup * ESUP:(sup + 1) * ESUP],
                             deps=[war, stg_war, mm_tail.get(sup - 2)])
                cp = S_.op("vector", lambda h, k=k, ib=ib, gu=gu:
                           h.tensor_copy((wgb if gu == 0 else wub)[ib][k][:],
                                         wstg[ib][:]),
                           deps=[ldw])
                if wdtype == I32:
                    cp = S_.op("vector", lambda h, k=k, ib=ib, gu=gu:
                               h.tensor_scalar_mul(
                                   (wgb if gu == 0 else wub)[ib][k][:],
                                   (wgb if gu == 0 else wub)[ib][k][:], wscale),
                               deps=[cp])
                cast_prev[(sup, gu, k)] = cp
                casts.append(cp)
        mm = None
        for ec in range(ESUP // 128):
            eabs = sup * ESUP + ec * 128
            for t in range(NTB):
                pb = (ec * NTB + t) % 2
                sl = slice(t * TB, (t + 1) * TB)
                mmg = mmu = None
                for k in range(DCH):
                    mmg = S_.op("tensor", lambda h, k=k, ib=ib, ec=ec, sl=sl,
                                pb=pb, st=(k == 0), sp=(k == DCH - 1):
                                h.matmul(ps_g[pb][:],
                                         wgb[ib][k][:, ec * 128:(ec + 1) * 128],
                                         xn_sb[k][:, sl], start=st, stop=sp),
                                deps=[cast_prev[(sup, 0, k)],
                                      xn_toks[k] if xn_toks else None,
                                      sil_prev[pb]])
                for k in range(DCH):
                    mmu = S_.op("tensor", lambda h, k=k, ib=ib, ec=ec, sl=sl,
                                pb=pb, st=(k == 0), sp=(k == DCH - 1):
                                h.matmul(ps_u[pb][:],
                                         wub[ib][k][:, ec * 128:(ec + 1) * 128],
                                         xn_sb[k][:, sl], start=st, stop=sp),
                                deps=[cast_prev[(sup, 1, k)], sil_prev[pb]])
                sil = S_.op("scalar", lambda h, pb=pb: h.activation(
                    gb[pb][:], ps_g[pb][:], AF.Silu), deps=[mmg, hb_free[pb]])
                hop = S_.op("vector", lambda h, pb=pb: h.scalar_tensor_tensor(
                    hb[pb][:], ps_u[pb][:], 1.0, gb[pb][:], ALU.mult, ALU.mult),
                    deps=[mmu, sil])
                sil_prev[pb] = hop
                wr = S_.dma("gpsimd", h_dram[eabs:eabs + 128, sl], hb[pb][:],
                            deps=[hop])
                hb_free[pb] = wr
                toks.append(wr)
                mm = mmu
        mm_tail[sup] = mm
    return toks


def emit_final_add(C, cst, x_dram, a_dram, f_dram, xout_dram,
                   ffn_f32=False):
    nc, S_ = C.nc, C.S
    xb = [C.psb(f"fa_x{i}", [128, S], F32) for i in range(2)]
    ab = [C.psb(f"fa_a{i}", [128, S], BF16) for i in range(2)]
    fb = [C.psb(f"fa_f{i}", [128, S], F32 if ffn_f32 else BF16)
          for i in range(2)]
    toks = []
    prev = {}
    for c in range(DCH):
        ib = c % 2
        war = prev.get(c - 2)
        l1 = S_.dma("sync", xb[ib][:], x_dram[c * 128:(c + 1) * 128, :],
                    deps=[war])
        l2 = S_.dma("sync", ab[ib][:], a_dram[c * 128:(c + 1) * 128, :],
                    deps=[war])
        l3 = S_.dma("sync", fb[ib][:], f_dram[c * 128:(c + 1) * 128, :],
                    deps=[war])
        t1 = S_.op("vector", lambda h, ib=ib: h.tensor_tensor(
            xb[ib][:], xb[ib][:], ab[ib][:], ALU.add), deps=[l1, l2])
        t2 = S_.op("vector", lambda h, ib=ib: h.tensor_tensor(
            xb[ib][:], xb[ib][:], fb[ib][:], ALU.add), deps=[l3, t1])
        wr = S_.dma("gpsimd", xout_dram[c * 128:(c + 1) * 128, :], xb[ib][:],
                    deps=[t2])
        prev[c] = wr
        toks.append(wr)
    return toks


def emit_layer2(C, cst, x_dram, xout_dram, w, scr):
    S_ = C.S
    C.push()
    xn_sb = [C.psb(f"l2_xn{c}", [128, S], BF16) for c in range(DCH)]
    xn_toks = emit_ln(C, cst, x_dram, xn_sb)
    C.push()
    bq = C.psb("l2_bq", [128, DCH], F32)
    b1 = S_.dma("sync", bq[:], w["l2_bq"][:])
    emit_proj2(C, cst, xn_sb, xn_toks, w["l2_wq"], D, scr["qT"], bq, b1)
    C.pop()
    C.push()
    bk = C.psb("l2_bk", [128, DCH], F32)
    b2 = S_.dma("sync", bk[:], w["l2_bk"][:])
    emit_proj2(C, cst, xn_sb, xn_toks, w["l2_wk"], D, scr["kT"], bk, b2)
    C.pop()
    C.push()
    emit_vtok(C, cst, xn_sb, xn_toks, w["l2_wv"], w["l2_bv"], scr["v_tok"])
    C.pop()
    C.push()
    emit_glu(C, cst, xn_sb, xn_toks, w["l2_wgT"], w["l2_wuT"], 3072,
             scr["h3072"], wdtype=I32, wscale=1.0 / 127.0)
    C.pop()
    C.pop()
    C.push()
    emit_proj_rhs(C, cst, scr["h3072"], w["l2_wdT"], D, scr["ffn"],
                  wdtype=I32, wscale=1.0 / 127.0)
    C.pop()
    emit_attn_l2(C, cst, scr["qT"], scr["kT"], scr["v_tok"], scr["o"])
    C.push()
    bo2 = C.psb("l2_bo2", [128, DCH], F32)
    S_.dma("sync", bo2[:], w["l2_bo"][:])
    emit_proj_rhs(C, cst, scr["o"], w["l2_wo"], D, scr["attn"], bias_sb=bo2)
    C.pop()
    C.push()
    emit_final_add(C, cst, x_dram, scr["attn"], scr["ffn"], xout_dram)
    C.pop()


def emit_proj2(C, cst, xn_sb, xn_toks, W_dram, E, out_dram, bias_sb, bias_tok,
               act=AF.Identity, scale=1.0):
    nc, S_ = C.nc, C.S
    Din = W_dram.shape[0]
    KCH = Din // 128
    ESUP = 512
    nsup = E // ESUP
    nm = W_dram.name
    wstg = [C.psb(f"p2_wstg{i}_{nm}", [128, ESUP], F32) for i in range(2)]
    wb = [[C.psb(f"p2_wb{i}k{k}_{nm}", [128, ESUP], BF16) for k in range(KCH)]
          for i in range(2)]
    ob = [C.psb(f"p2_ob{i}_{nm}", [128, TB], BF16) for i in range(2)]
    psum = [C.pps(f"p2_ps{i}_{nm}", [128, TB]) for i in range(2)]
    cast_prev = {}
    mm_tail = {}
    evac_prev = [None, None]
    ob_free = [None, None]
    toks = []
    for sup in range(nsup):
        ib = sup % 2
        for k in range(KCH):
            stg_war = (cast_prev.get((sup, k - 1)) if k
                       else cast_prev.get((sup - 2, KCH - 1)))
            ldw = S_.dma("sync", wstg[ib][:],
                         W_dram[k * 128:(k + 1) * 128,
                                sup * ESUP:(sup + 1) * ESUP],
                         deps=[stg_war, mm_tail.get(sup - 2)])
            cast_prev[(sup, k)] = S_.op(
                "vector", lambda h, k=k, ib=ib: h.tensor_copy(wb[ib][k][:],
                                                              wstg[ib][:]),
                deps=[ldw])
        mm = None
        for ec in range(ESUP // 128):
            eabs = sup * ESUP + ec * 128
            for t in range(NTB):
                pb = (ec * NTB + t) % 2
                sl = slice(t * TB, (t + 1) * TB)
                for k in range(KCH):
                    mm = S_.op("tensor", lambda h, k=k, ib=ib, ec=ec, sl=sl,
                               pb=pb, st=(k == 0), sp=(k == KCH - 1):
                               h.matmul(psum[pb][:],
                                        wb[ib][k][:, ec * 128:(ec + 1) * 128],
                                        xn_sb[k][:, sl], start=st, stop=sp),
                               deps=[cast_prev[(sup, k)],
                                     xn_toks[k] if xn_toks else None,
                                     evac_prev[pb]])
                ev = S_.op("scalar", lambda h, pb=pb, ecol=eabs // 128:
                           h.activation(ob[pb][:], psum[pb][:], act,
                                        bias=(bias_sb[:, ecol:ecol + 1]
                                              if bias_sb is not None else 0.0),
                                        scale=scale),
                           deps=[mm, bias_tok, ob_free[pb]])
                evac_prev[pb] = ev
                wr = S_.dma("gpsimd", out_dram[eabs:eabs + 128, sl], ob[pb][:],
                            deps=[ev])
                ob_free[pb] = wr
                toks.append(wr)
        mm_tail[sup] = mm
    return toks


def colbias(b):
    return np.ascontiguousarray(b.reshape(-1, 128).T)


def prep_l2(inp):
    w = {}
    w["l2_wq"] = np.ascontiguousarray(inp["L2_attn_Wq"])
    w["l2_wk"] = np.ascontiguousarray(inp["L2_attn_Wk"])
    w["l2_wv"] = np.ascontiguousarray(inp["L2_attn_Wv"])
    w["l2_wo"] = np.ascontiguousarray(inp["L2_attn_Wo"])
    w["l2_bq"] = colbias(inp["L2_attn_bq"])
    w["l2_bk"] = colbias(inp["L2_attn_bk"])
    w["l2_bo"] = colbias(inp["L2_attn_bo"])
    w["l2_bv"] = np.ascontiguousarray(inp["L2_attn_bv"].reshape(1, -1))
    w["l2_wgT"] = np.ascontiguousarray(inp["L2_ffn_w_gate"].T)
    w["l2_wuT"] = np.ascontiguousarray(inp["L2_ffn_w_up"].T)
    w["l2_wdT"] = np.ascontiguousarray(inp["L2_ffn_w_down"].T)
    return w


L2_WSPECS = [
    ("l2_wq", [D, D], F32), ("l2_wk", [D, D], F32), ("l2_wv", [D, D], F32),
    ("l2_wo", [D, D], F32), ("l2_bq", [128, DCH], F32),
    ("l2_bk", [128, DCH], F32), ("l2_bo", [128, DCH], F32),
    ("l2_bv", [1, D], F32), ("l2_wgT", [D, 3072], I32),
    ("l2_wuT", [D, 3072], I32), ("l2_wdT", [3072, D], I32),
]


def declare_l2(nc):
    w = {}
    for name, shape, dt in L2_WSPECS:
        w[name] = nc.dram_tensor(name, shape, dt, kind="ExternalInput").ap()
    return w


def declare_scratch_l2(nc, tag=""):
    scr = {}
    scr["qT"] = nc.dram_tensor(f"s{tag}_qT", [D, S], BF16, kind="Internal").ap()
    scr["kT"] = nc.dram_tensor(f"s{tag}_kT", [D, S], BF16, kind="Internal").ap()
    scr["v_tok"] = nc.dram_tensor(f"s{tag}_vtok", [S, D], BF16, kind="Internal").ap()
    scr["o"] = nc.dram_tensor(f"s{tag}_o", [D, S], BF16, kind="Internal").ap()
    scr["attn"] = nc.dram_tensor(f"s{tag}_attn", [D, S], BF16, kind="Internal").ap()
    scr["ffn"] = nc.dram_tensor(f"s{tag}_ffn", [D, S], BF16, kind="Internal").ap()
    scr["h3072"] = nc.dram_tensor(f"s{tag}_h3072", [3072, S], BF16, kind="Internal").ap()
    return scr


def finish(C):
    S_ = C.S
    S_.gate()
    S_.op("gpsimd", lambda h: h.engine_nop())


def emit_gamma_recip(C, cst, W_dram, rcol_out, dbg=None):
    nc, S_ = C.nc, C.S
    K, E = W_dram.shape
    KCH = K // 128
    C.push()
    stg = [C.psb("gm_stg", [128, E], F32) for i in range(2)]
    ab = C.psb("gm_ab", [128, E], BF16)
    row = C.psb("gm_row", [1, E], F32)
    sc = C.psb("gm_sc", [1, 8], F32)
    C.push()
    psn = C.pps("gm_psn", [1, E])
    NSL = E // TB
    ab_done = None
    mm = [None] * NSL
    for k in range(KCH):
        buf = stg[k % 2]
        ld = S_.dma("sync", buf[:], W_dram[k * 128:(k + 1) * 128, :],
                    deps=[mm[-1] if k >= 2 else None])
        ab_done = S_.op("scalar", lambda h, b=buf: h.activation(
            ab[:], b[:], AF.Abs), deps=[ld, mm[-1] if k else None])
        for t in range(NSL):
            mm[t] = S_.op("tensor", lambda h, t=t, st=(k == 0), sp=(k == KCH - 1):
                          h.matmul(psn[:, t * TB:(t + 1) * TB],
                                   cst["ones_col_bf"][:],
                                   ab[:, t * TB:(t + 1) * TB],
                                   start=st, stop=sp),
                          deps=[ab_done])
    e1 = S_.op("vector", lambda h: h.tensor_copy(row[:], psn[:]), deps=[mm[-1]])
    C.pop()
    psb = C.pps("gm_psb", [128, 1])
    r1 = S_.op("vector", lambda h: h.tensor_reduce(
        sc[:, 0:1].unsqueeze(2), row.rearrange("a (b c) -> a b c", b=1),
        AX.X, ALU.add), deps=[e1])
    r2 = S_.op("gpsimd", lambda h: h.tensor_scalar(
        sc[:, 1:2], sc[:, 0:1], 1.0 / (K * E), 1e-5, ALU.mult, ALU.add),
        deps=[r1])
    r3 = S_.op("vector", lambda h: h.reciprocal(sc[:, 2:3], sc[:, 1:2]),
               deps=[r2])
    m1 = S_.op("tensor", lambda h: h.matmul(
        psb[:], cst["ones_row_f"][:], sc[:, 2:3], start=True, stop=True),
        deps=[r3])
    r4 = S_.op("vector", lambda h: h.tensor_copy(rcol_out[:], psb[:]),
               deps=[m1])
    if dbg is not None:
        S_.dma("gpsimd", dbg[0][:], sc[:], deps=[r4])
        S_.dma("gpsimd", dbg[1][:], row[:, 0:1024], deps=[r4])
    C.pop()
    return r4


def emit_quant_tile(C, S_, src_f32, dst_bf, rcol, deps, tmp_u, tmp_s):
    o1 = S_.op("vector", lambda h: h.tensor_scalar_mul(tmp_u, src_f32, rcol),
               deps=deps)
    o2 = S_.op("scalar", lambda h: h.activation(dst_bf, tmp_u, AF.Sign),
               deps=[o1])
    o3 = S_.op("scalar", lambda h: h.activation(tmp_u, tmp_u, AF.Abs),
               deps=[o2])
    o4 = S_.op("vector", lambda h: h.tensor_single_scalar(
        tmp_s, tmp_u, 0.5, ALU.is_ge), deps=[o3])
    o5 = S_.op("vector", lambda h: h.tensor_tensor(dst_bf, dst_bf, tmp_s,
                                                   ALU.mult), deps=[o4])
    return o5


def emit_proj_bitnet(C, cst, xn_sb, xn_toks, W_dram, E, out_dram, rcol, rtok,
                     bias_sb=None, bias_tok=None, act=AF.Identity):
    nc, S_ = C.nc, C.S
    Din = W_dram.shape[0]
    KCH = Din // 128
    ESUP = 512
    nsup = E // ESUP
    nm = W_dram.name
    wstg = [C.psb(f"pb_wstg{i}_{nm}", [128, ESUP], F32) for i in range(2)]
    wtmp = C.psb(f"pb_wtmp_{nm}", [128, ESUP], F32)
    wts = C.psb(f"pb_wts_{nm}", [128, ESUP], BF16)
    wb = [[C.psb(f"pb_wb{i}k{k}_{nm}", [128, ESUP], BF16) for k in range(KCH)]
          for i in range(2)]
    ob = [C.psb(f"pb_ob{i}_{nm}", [128, TB], BF16) for i in range(2)]
    psum = [C.pps(f"pb_ps{i}_{nm}", [128, TB]) for i in range(2)]
    cast_prev = {}
    mm_tail = {}
    evac_prev = [None, None]
    ob_free = [None, None]
    toks = []
    for sup in range(nsup):
        ib = sup % 2
        for k in range(KCH):
            stg_war = (cast_prev.get((sup, k - 1)) if k
                       else cast_prev.get((sup - 2, KCH - 1)))
            ldw = S_.dma("sync", wstg[ib][:],
                         W_dram[k * 128:(k + 1) * 128,
                                sup * ESUP:(sup + 1) * ESUP],
                         deps=[stg_war, mm_tail.get(sup - 2)])
            cast_prev[(sup, k)] = emit_quant_tile(
                C, S_, wstg[ib][:], wb[ib][k][:], rcol[:, 0:1],
                [ldw, rtok, stg_war], wtmp[:], wts[:])
        mm = None
        for ec in range(ESUP // 128):
            eabs = sup * ESUP + ec * 128
            for t in range(NTB):
                pb = (ec * NTB + t) % 2
                sl = slice(t * TB, (t + 1) * TB)
                for k in range(KCH):
                    mm = S_.op("tensor", lambda h, k=k, ib=ib, ec=ec, sl=sl,
                               pb=pb, st=(k == 0), sp=(k == KCH - 1):
                               h.matmul(psum[pb][:],
                                        wb[ib][k][:, ec * 128:(ec + 1) * 128],
                                        xn_sb[k][:, sl], start=st, stop=sp),
                               deps=[cast_prev[(sup, k)],
                                     xn_toks[k] if xn_toks else None,
                                     evac_prev[pb]])
                ev = S_.op("scalar", lambda h, pb=pb, ecol=eabs // 128:
                           h.activation(ob[pb][:], psum[pb][:], act,
                                        bias=(bias_sb[:, ecol:ecol + 1]
                                              if bias_sb is not None else 0.0)),
                           deps=[mm, bias_tok, ob_free[pb]])
                evac_prev[pb] = ev
                wr = S_.dma("gpsimd", out_dram[eabs:eabs + 128, sl], ob[pb][:],
                            deps=[ev])
                ob_free[pb] = wr
                toks.append(wr)
        mm_tail[sup] = mm
    return toks


def emit_proj_rhs_bitnet(C, cst, rhs_dram, W_dram, E, out_dram, rcol, rtok,
                         bias_sb=None, bias_tok=None, act=AF.Identity):
    nc, S_ = C.nc, C.S
    K = W_dram.shape[0]
    KCH = K // 128
    ECH = E // 128
    nm = W_dram.name
    wstg = C.psb(f"pq_wstg_{nm}", [128, E], F32)
    wtmp = C.psb(f"pq_wtmp_{nm}", [128, E], F32)
    wts = C.psb(f"pq_wts_{nm}", [128, E], BF16)
    wsb = [C.psb(f"pq_w{k}_{nm}", [128, E], BF16) for k in range(KCH)]
    rsb = [[C.psb(f"pq_r{i}k{k}_{nm}", [128, TB], BF16) for k in range(KCH)]
           for i in range(2)]
    ob = [C.psb(f"pq_ob{i}_{nm}", [128, TB], BF16) for i in range(2)]
    psum = [C.pps(f"pq_ps{i}_{nm}", [128, TB]) for i in range(2)]
    casts = []
    prev_cast = None
    for k in range(KCH):
        ldw = S_.dma("sync", wstg[:], W_dram[k * 128:(k + 1) * 128, :],
                     deps=[prev_cast])
        prev_cast = emit_quant_tile(C, S_, wstg[:], wsb[k][:], rcol[:, 0:1],
                                    [ldw, rtok], wtmp[:], wts[:])
        casts.append(prev_cast)
    evac_prev = [None, None]
    ob_free = [None, None]
    ld_prev = {}
    toks = []
    for t in range(NTB):
        ib = t % 2
        sl = slice(t * TB, (t + 1) * TB)
        lds = []
        for k in range(KCH):
            war = ld_prev.get((t - 2, k))
            lds.append(S_.dma("sync", rsb[ib][k][:],
                              rhs_dram[k * 128:(k + 1) * 128, sl], deps=[war]))
        for ec in range(ECH):
            pb = (t * ECH + ec) % 2
            mm = None
            for k in range(KCH):
                mm = S_.op("tensor", lambda h, k=k, ib=ib, ec=ec, pb=pb,
                           st=(k == 0), sp=(k == KCH - 1):
                           h.matmul(psum[pb][:], wsb[k][:, ec * 128:(ec + 1) * 128],
                                    rsb[ib][k][:], start=st, stop=sp),
                           deps=[casts[k], lds[k], evac_prev[pb]])
            for k in range(KCH):
                ld_prev[(t, k)] = mm
            ev = S_.op("scalar", lambda h, pb=pb, ec=ec:
                       h.activation(ob[pb][:], psum[pb][:], act,
                                    bias=(bias_sb[:, ec:ec + 1]
                                          if bias_sb is not None else 0.0)),
                       deps=[mm, bias_tok, ob_free[pb]])
            evac_prev[pb] = ev
            wr = S_.dma("gpsimd", out_dram[ec * 128:(ec + 1) * 128, sl],
                        ob[pb][:], deps=[ev])
            ob_free[pb] = wr
            toks.append(wr)
    return toks


def emit_rope(C, cst, q_dram, out_dram, cos_dram, sin_dram):
    nc, S_ = C.nc, C.S
    C.push()
    qb = [C.psb("rp_q", [128, S], BF16) for _ in range(2)]
    sw = C.psb("rp_sw", [128, S], BF16)
    t1 = C.psb("rp_t1", [128, S], F32)
    cosb = C.psb("rp_cos", [128, S], F32)
    sinb = C.psb("rp_sin", [128, S], F32)
    lc = S_.dma("sync", cosb[:], cos_dram[:])
    ls = S_.dma("sync", sinb[:], sin_dram[:])
    prev = {}
    for c in range(DCH):
        buf = qb[c % 2]
        war = prev.get(c - 2)
        ld = S_.dma("sync", buf[:], q_dram[c * 128:(c + 1) * 128, :],
                    deps=[war])
        sw1 = S_.dma("gpsimd", sw[0:32, :], buf[32:64, :], deps=[ld, war])
        sw2 = S_.dma("gpsimd", sw[32:64, :], buf[0:32, :], deps=[ld, war])
        sw3 = S_.dma("gpsimd", sw[64:96, :], buf[96:128, :], deps=[ld, war])
        sw4 = S_.dma("gpsimd", sw[96:128, :], buf[64:96, :], deps=[ld, war])
        o1 = S_.op("vector", lambda h, b=buf: h.tensor_tensor(
            t1[:], b[:], cosb[:], ALU.mult), deps=[ld, lc, prev.get(c - 1)])
        o2 = S_.op("vector", lambda h: h.tensor_tensor(
            sw[:], sw[:], sinb[:], ALU.mult),
            deps=[sw1, sw2, sw3, sw4, ls])
        o3 = S_.op("vector", lambda h, b=buf: h.tensor_tensor(
            b[:], t1[:], sw[:], ALU.add), deps=[o1, o2])
        wr = S_.dma("sync", out_dram[c * 128:(c + 1) * 128, :], buf[:],
                    deps=[o3])
        prev[c] = wr
    C.pop()


def emit_attn_l1(C, cst, qT_dram, kT_dram, v_tok_dram, o_dram):
    nc, S_ = C.nc, C.S
    UCH = S // 128
    negmb = C.sb("a1_negmb", [128, 128], F32)
    C.push()
    xs = [C.psb("a1p_x", [128, S], BF16) for _ in range(2)]
    sqt = C.psb("a1p_sq", [128, S], BF16)
    qmx = C.psb("a1p_qmx", [1, 128], F32)
    kmx = C.psb("a1p_kmx", [1, 128], F32)
    kmc = C.psb("a1p_kmc", [1, 16], F32)
    psnA = C.pps("a1p_psnA", [1, TB])
    psnB = C.pps("a1p_psnB", [1, TB])
    sq_last = None
    ev_prev = None
    qn_done = kn_done = None
    for which, (src_d, rows) in enumerate(((qT_dram, qmx), (kT_dram, kmx))):
        for c in range(DCH):
            buf = xs[c % 2]
            ld = S_.dma("sync", buf[:], src_d[c * 128:(c + 1) * 128, :],
                        deps=[sq_last])
            sq = S_.op("scalar", lambda h, b=buf: h.activation(
                sqt[:], b[:], AF.Square), deps=[ld, ev_prev])
            sq_last = sq
            for t in range(NTB):
                sl = slice(t * TB, (t + 1) * TB)
                colA = 0 * 64 + c * NTB + t
                colB = 1 * 64 + c * NTB + t
                mmA = S_.op("tensor", lambda h, sl=sl: h.matmul(
                    psnA[:], cst["ones_col_bf"][0:64, :], sqt[0:64, sl],
                    start=True, stop=True), deps=[sq, ev_prev])
                mmB = S_.op("tensor", lambda h, sl=sl: h.matmul(
                    psnB[:], cst["ones_col_bf"][64:128, :], sqt[64:128, sl],
                    start=True, stop=True, tile_position=(64, 0)),
                    deps=[sq, ev_prev])
                e1 = S_.op("vector",
                           lambda h, colA=colA, rows=rows:
                           h.reduce_max(rows[:, colA:colA + 1], psnA[:],
                                        axis=AX.X), deps=[mmA])
                ev_prev = S_.op("vector",
                                lambda h, colB=colB, rows=rows:
                                h.reduce_max(rows[:, colB:colB + 1], psnB[:],
                                             axis=AX.X), deps=[mmB, e1])
        if which == 0:
            qn_done = ev_prev
        else:
            kn_done = ev_prev
    r1 = S_.op("scalar", lambda h: h.activation(qmx[:], qmx[:], AF.Sqrt),
               deps=[qn_done])
    r2 = S_.op("scalar", lambda h: h.activation(kmx[:], kmx[:], AF.Sqrt),
               deps=[kn_done, r1])
    r3 = S_.op("vector", lambda h: h.tensor_reduce(
        kmc.rearrange("a (hh c) -> a hh c", hh=2).unsqueeze(3),
        kmx.rearrange("a (hh c t) -> a hh c t", hh=2, t=NTB),
        AX.X, ALU.max), deps=[r2])
    r3b = S_.op("scalar", lambda h: h.activation(kmc[:], kmc[:], AF.Identity),
                deps=[r3])
    r4 = S_.op("vector", lambda h: h.scalar_tensor_tensor(
        qmx.rearrange("a (hh c t) -> a hh c t", hh=2, t=NTB),
        qmx.rearrange("a (hh c t) -> a hh c t", hh=2, t=NTB), -0.125,
        kmc.rearrange("a (hh c) -> a hh c", hh=2).unsqueeze(3)
           .broadcast_to([1, 2, DCH, NTB]),
        ALU.mult, ALU.mult), deps=[r3b, r1])
    psb2 = C.pps("a1_psb2", [128, 128])
    m1 = S_.op("tensor", lambda h: h.matmul(
        psb2[:], cst["ones_row_f"][:], qmx[:], start=True, stop=True),
        deps=[r4])
    S_.op("vector", lambda h: h.tensor_copy(negmb[:], psb2[:]), deps=[m1])
    C.pop()

    C.phase_open()
    qsb = C.psb("a1_q", [128, S], BF16)
    ksb = C.psb("a1_k", [128, S], BF16)
    vp = [C.psb(f"a1_vp{u}", [128, 130], BF16) for u in range(UCH)]
    pt = [C.psb(f"a1_pt{u}", [128, TB], BF16) for u in range(UCH)]
    rl = C.psb("a1_rl", [1, TB], F32)
    rlb = C.psb("a1_rlb", [64, TB], F32)
    obuf = [C.psb(f"a1_ob{i}", [64, TB], BF16) for i in range(2)]
    vstg = C.psb("a1_vstg", [128, 128], BF16)
    ps_s = [C.pps(f"a1_pss{i}", [128, TB]) for i in range(2)]
    ps_o = [C.pps(f"a1_pso{i}", [65, TB]) for i in range(2)]
    ps_b = C.pps("a1_psb", [64, TB])
    for c in range(DCH):
        S_.gate()
        lq = S_.dma("sync", qsb[:], qT_dram[c * 128:(c + 1) * 128, :])
        lk = S_.dma("sync", ksb[:], kT_dram[c * 128:(c + 1) * 128, :])
        vplus_tok = {}
        vprev = None
        for u in range(UCH):
            lv = S_.dma("sync", vstg[:],
                        v_tok_dram[u * 128:(u + 1) * 128,
                                   c * 128:(c + 1) * 128], deps=[vprev])
            c1 = S_.op("vector", lambda h, u=u: h.tensor_copy(
                vp[u][:, 0:64], vstg[:, 0:64]), deps=[lv])
            c2 = S_.op("vector", lambda h, u=u: h.tensor_copy(
                vp[u][:, 65:129], vstg[:, 64:128]), deps=[lv])
            c3 = S_.op("vector", lambda h, u=u: h.memset(vp[u][:, 64:65], 1.0))
            c4 = S_.op("vector", lambda h, u=u: h.memset(vp[u][:, 129:130], 1.0))
            vprev = c2
            vplus_tok[u] = c4
        sbank_last = [None, None]
        obank_last = [None, None]
        pt_reader = {}
        m2_prev = None
        e3_prev = None
        obfree = [None, None]
        for hh in range(2):
            head = 2 * c + hh
            hsl = slice(64 * hh, 64 * hh + 64)
            tp = (64 * hh, 0)
            for j in range(NTB):
                numu = 4 * (j + 1)
                expd = {}
                for u in range(numu):
                    sb = u % 2
                    mm = S_.op("tensor", lambda h, u=u, j=j, hsl=hsl, tp=tp,
                               sb=sb:
                               h.matmul(ps_s[sb][:],
                                        ksb[hsl, u * 128:(u + 1) * 128],
                                        qsb[hsl, j * TB:(j + 1) * TB],
                                        start=True, stop=True,
                                        tile_position=tp),
                               deps=[lq, lk, sbank_last[sb]])
                    ex = S_.op("scalar", lambda h, u=u, sb=sb, head=head, j=j:
                               h.activation(pt[u][:], ps_s[sb][:], AF.Exp,
                                            scale=0.125,
                                            bias=negmb[:, hh * 64 + (head // 2) * 8 + j:hh * 64 + (head // 2) * 8 + j + 1]),
                               deps=[mm, pt_reader.get(u)])
                    sbank_last[sb] = ex
                    if u >= numu - 4:
                        ex = S_.op("gpsimd", lambda h, u=u, j=j:
                                   h.affine_select(
                                       out=pt[u][:], in_=pt[u][:],
                                       pattern=[[1, TB]],
                                       compare_op=ALU.is_ge, fill=0.0,
                                       base=j * TB - u * 128,
                                       channel_multiplier=-1),
                                   deps=[ex])
                    expd[u] = ex
                ob = j % 2
                mm = None
                for u in range(numu):
                    mm = S_.op("tensor", lambda h, u=u, hh=hh, ob=ob,
                               st=(u == 0), sp=(u == numu - 1):
                               h.matmul(ps_o[ob][:],
                                        vp[u][:, 65 * hh:65 * hh + 65],
                                        pt[u][:], start=st, stop=sp),
                               deps=[vplus_tok[u], expd[u], obank_last[ob]])
                    pt_reader[u] = mm
                e1 = S_.op("vector", lambda h, ob=ob: h.tensor_copy(
                    rl[:], ps_o[ob][64:65, :]), deps=[mm, m2_prev])
                e2 = S_.op("vector", lambda h: h.reciprocal(rl[:], rl[:]),
                           deps=[e1])
                m2 = S_.op("tensor", lambda h: h.matmul(
                    ps_b[:], cst["ones_row_f"][:, 0:64], rl[:],
                    start=True, stop=True), deps=[e2, e3_prev])
                m2_prev = m2
                e3 = S_.op("vector", lambda h: h.tensor_copy(rlb[:], ps_b[:]),
                           deps=[m2])
                e3_prev = e3
                ev = S_.op("vector", lambda h, ob=ob: h.tensor_tensor(
                    obuf[ob][:], ps_o[ob][0:64, :], rlb[:], ALU.mult),
                    deps=[e3, obfree[ob]])
                obank_last[ob] = ev
                wr = S_.dma("gpsimd",
                            o_dram[c * 128 + 64 * hh:c * 128 + 64 * hh + 64,
                                   j * TB:(j + 1) * TB],
                            obuf[ob][:], deps=[ev])
                obfree[ob] = wr
    C.phase_close()


def emit_layer1(C, cst, x_dram, xout_dram, w, scr):
    S_ = C.S
    C.push()
    xn_sb = [C.psb(f"l1_xn{c}", [128, S], BF16) for c in range(DCH)]
    xn_toks = emit_ln(C, cst, x_dram, xn_sb)
    C.push()
    emit_proj2(C, cst, xn_sb, xn_toks, w["l1_wqkvT"], 2048, scr["qkT"],
               None, None)
    C.pop()
    C.push()
    emit_vtok(C, cst, xn_sb, xn_toks, w["l1_wvT"], None, scr["v_tok"])
    C.pop()
    C.push()
    rc1 = C.psb("l1_rc1", [128, 1], F32)
    rt1 = emit_gamma_recip(C, cst, w["l1_w1T"], rc1)
    C.push()
    bs1 = C.psb("l1_b1", [128, 32], F32)
    bt1 = S_.dma("sync", bs1[:], w["l1_b1"][:])
    emit_proj_bitnet(C, cst, xn_sb, xn_toks, w["l1_w1T"], 4096, scr["h4096"],
                     rc1, rt1, bias_sb=bs1, bias_tok=bt1, act=AF.Relu)
    C.pop()
    C.pop()
    C.pop()
    C.push()
    rc2b = C.psb("l1_rc2b", [128, 1], F32)
    rt2b = emit_gamma_recip(C, cst, w["l1_w2T"], rc2b)
    bs2 = C.psb("l1_b2", [128, 8], F32)
    bt2 = S_.dma("sync", bs2[:], w["l1_b2"][:])
    emit_proj_rhs_bitnet(C, cst, scr["h4096"], w["l1_w2T"], D, scr["ffn"],
                         rc2b, rt2b, bias_sb=bs2, bias_tok=bt2)
    C.pop()
    emit_rope(C, cst, scr["qkT"][0:D, :], scr["qT"], w["l1_cos"], w["l1_sin"])
    emit_rope(C, cst, scr["qkT"][D:2 * D, :], scr["kT"], w["l1_cos"],
              w["l1_sin"])
    emit_attn_l1(C, cst, scr["qT"], scr["kT"], scr["v_tok"], scr["o"])
    C.push()
    emit_proj_rhs(C, cst, scr["o"], w["l1_woT"], D, scr["attn"])
    C.pop()
    C.push()
    emit_final_add(C, cst, x_dram, scr["attn"], scr["ffn"], xout_dram)
    C.pop()


def prep_l1(inp):
    w = {}
    Wqkv = inp["L1_attn_W_qkv"]
    WT = np.ascontiguousarray(Wqkv.T)
    perm = np.arange(3072)
    for part in range(2):
        for hh in range(16):
            base = part * 1024 + hh * 64
            perm[base:base + 64] = np.concatenate(
                [np.arange(base, base + 64, 2), np.arange(base + 1, base + 64, 2)])
    w["l1_wqkvT"] = np.ascontiguousarray(WT[:, perm[:2048]])
    w["l1_wvT"] = np.ascontiguousarray(WT[:, 2048:])
    w["l1_woT"] = np.ascontiguousarray(inp["L1_attn_W_out"].T)
    w["l1_w1T"] = np.ascontiguousarray(inp["L1_ffn_W1"].T)
    w["l1_w2T"] = np.ascontiguousarray(inp["L1_ffn_W2"].T)
    w["l1_b1"] = colbias(inp["L1_ffn_b1"])
    w["l1_b2"] = colbias(inp["L1_ffn_b2"])
    t = np.arange(S, dtype=np.float64)
    freqs = 1.0 / 10000.0 ** (np.arange(0, 64, 2, dtype=np.float64) / 64.0)
    ang = np.outer(freqs, t)
    cos = np.cos(ang).astype(np.float32)
    sin = np.sin(ang).astype(np.float32)
    C64 = np.concatenate([cos, cos], axis=0)
    S64 = np.concatenate([-sin, sin], axis=0)
    w["l1_cos"] = np.ascontiguousarray(np.concatenate([C64, C64], axis=0))
    w["l1_sin"] = np.ascontiguousarray(np.concatenate([S64, S64], axis=0))
    return w


L1_WSPECS = [
    ("l1_wqkvT", [D, 2048], F32), ("l1_wvT", [D, D], F32),
    ("l1_woT", [D, D], F32), ("l1_w1T", [D, 4096], F32),
    ("l1_w2T", [4096, D], F32), ("l1_b1", [128, 32], F32),
    ("l1_b2", [128, 8], F32), ("l1_cos", [128, S], F32),
    ("l1_sin", [128, S], F32),
]


def declare_l1(nc):
    w = {}
    for name, shape, dt in L1_WSPECS:
        w[name] = nc.dram_tensor(name, shape, dt, kind="ExternalInput").ap()
    return w


def declare_scratch_l1(nc, tag=""):
    scr = {}
    scr["qkT"] = nc.dram_tensor(f"s{tag}_qkT", [2048, S], BF16, kind="Internal").ap()
    scr["qT"] = nc.dram_tensor(f"s{tag}_l1qT", [D, S], BF16, kind="Internal").ap()
    scr["kT"] = nc.dram_tensor(f"s{tag}_l1kT", [D, S], BF16, kind="Internal").ap()
    scr["v_tok"] = nc.dram_tensor(f"s{tag}_l1vtok", [S, D], BF16, kind="Internal").ap()
    scr["o"] = nc.dram_tensor(f"s{tag}_l1o", [D, S], BF16, kind="Internal").ap()
    scr["attn"] = nc.dram_tensor(f"s{tag}_l1attn", [D, S], BF16, kind="Internal").ap()
    scr["ffn"] = nc.dram_tensor(f"s{tag}_l1ffn", [D, S], BF16, kind="Internal").ap()
    scr["h4096"] = nc.dram_tensor(f"s{tag}_l1h", [4096, S], BF16, kind="Internal").ap()
    return scr


def emit_fp4_tile(C_, S_, src, dst, P, Ccols, scr, halfcol, deps):
    a, un, sc, rc, c1, c2, tt, m2 = (scr["a"], scr["un"], scr["sc"],
                                     scr["rc"], scr["c1"], scr["c2"],
                                     scr["t"], scr["m2"])
    NB = Ccols // 16
    o1 = S_.op("scalar", lambda h: h.activation(a, src, AF.Abs), deps=deps)
    o2 = S_.op("vector", lambda h: h.tensor_reduce(
        sc.rearrange("p c -> p c").unsqueeze(2),
        a.rearrange("p (b s) -> p b s", s=16), AX.X, ALU.max), deps=[o1])
    o3 = S_.op("gpsimd", lambda h: h.tensor_scalar(
        sc, sc, 1.0 / 6.0, 1e-12, ALU.mult, ALU.max), deps=[o2])
    o4 = S_.op("vector", lambda h: h.reciprocal(rc, sc), deps=[o3])
    o5 = S_.op("vector", lambda h: h.tensor_tensor(
        un.rearrange("p (b s) -> p b s", s=16),
        a.rearrange("p (b s) -> p b s", s=16),
        rc.unsqueeze(2).broadcast_to([P, NB, 16]), ALU.mult), deps=[o4])
    o6 = S_.op("scalar", lambda h: h.activation(
        c1, un, AF.Identity, scale=2.0), deps=[o5])
    o7 = S_.op("scalar", lambda h: h.activation(
        c2, un, AF.Identity, scale=1.0), deps=[o6])
    o8 = S_.op("vector", lambda h: h.tensor_scalar(
        tt, un, 5.0, 2.0, ALU.is_gt, ALU.mult), deps=[o7])
    o9 = S_.op("vector", lambda h: h.scalar_tensor_tensor(
        m2, c2, 4.0, tt, ALU.min, ALU.add), deps=[o8])
    o9b = S_.op("vector", lambda h: h.tensor_single_scalar(
        c2, un, 2.25, ALU.is_gt), deps=[o9])
    o9c = S_.op("vector", lambda h: h.tensor_tensor(m2, m2, c2, ALU.mult),
                deps=[o9b])
    o10 = S_.op("vector", lambda h: h.tensor_scalar(
        tt, c1, 0.5, 2.0, ALU.mult, ALU.min), deps=[o9c])
    o11 = S_.op("vector", lambda h: h.tensor_tensor(m2, m2, tt, ALU.max),
                deps=[o10])
    o12 = S_.op("scalar", lambda h: h.activation(a, src, AF.Sign), deps=[o11])
    o13 = S_.op("vector", lambda h: h.tensor_tensor(
        un.rearrange("p (b s) -> p b s", s=16),
        a.rearrange("p (b s) -> p b s", s=16),
        sc.unsqueeze(2).broadcast_to([P, NB, 16]), ALU.mult), deps=[o12])
    o14 = S_.op("vector", lambda h: h.tensor_tensor(dst, m2, un, ALU.mult),
                deps=[o13])
    return o14


def mk_fp4_scratch(C, Ccols, tag=""):
    scr = {}
    scr["a"] = C.psb(f"f4a{tag}", [128, Ccols], F32)[:]
    scr["un"] = C.psb(f"f4u{tag}", [128, Ccols], F32)[:]
    scr["sc"] = C.psb(f"f4s{tag}", [128, Ccols // 16], F32)[:]
    scr["rc"] = C.psb(f"f4r{tag}", [128, Ccols // 16], F32)[:]
    scr["c1"] = C.psb(f"f4c1{tag}", [128, Ccols], I32)[:]
    scr["c2"] = C.psb(f"f4c2{tag}", [128, Ccols], I32)[:]
    scr["t"] = C.psb(f"f4t{tag}", [128, Ccols], F32)[:]
    scr["m2"] = C.psb(f"f4m{tag}", [128, Ccols], F32)[:]
    return scr


def emit_wq_prep(C, cst, W_dram, out_dram, halfcol):
    nc, S_ = C.nc, C.S
    R, Ccols = W_dram.shape
    C.push()
    stg = [C.psb("wq_stg", [128, Ccols], F32) for _ in range(2)]
    qd = [C.psb("wq_qd", [128, Ccols], BF16) for _ in range(2)]
    ot = C.psb("wq_ot", [128, 128], BF16)
    scr = mk_fp4_scratch(C, Ccols)
    pst = C.pps("wq_pst", [128, 128], BF16)
    prev_q = {}
    ev_prev = None
    for r in range(R // 128):
        buf = stg[r % 2]
        qb = qd[r % 2]
        ld = S_.dma("sync", buf[:], W_dram[r * 128:(r + 1) * 128, :],
                    deps=[prev_q.get(r - 2)])
        qt = emit_fp4_tile(C, S_, buf[:], qb[:], 128, Ccols, scr, halfcol,
                           [ld, prev_q.get(r - 1), prev_q.get(r - 2)])
        prev_q[r] = qt
        for cc in range(Ccols // 128):
            mt = S_.op("tensor", lambda h, qb=qb, cc=cc: h.transpose(
                pst[:], qb[:, cc * 128:(cc + 1) * 128], cst["ident_bf"][:]),
                deps=[qt, ev_prev])
            ev = S_.op("vector", lambda h: h.tensor_copy(ot[:], pst[:]),
                       deps=[mt])
            ev_prev = ev
            wr = S_.dma("gpsimd", out_dram[cc * 128:(cc + 1) * 128,
                                           r * 128:(r + 1) * 128],
                        ot[:], deps=[ev])
            ev_prev = wr
    C.pop()


def emit_attn_l3(C, cst, qkvT_dram, gT_dram, attn_dram):
    nc, S_ = C.nc, C.S
    C.push()
    qb = C.psb("a3_q", [128, S], BF16)
    kb = C.psb("a3_k", [128, S], BF16)
    vb = C.psb("a3_v", [128, S], BF16)
    gb = C.psb("a3_g", [128, S], BF16)
    kv = C.psb("a3_kv", [128, S], F32)
    zc = C.psb("a3_zc", [128, 1], F32)
    ob = C.psb("a3_ob", [128, S], BF16)
    z0 = S_.op("gpsimd", lambda h: h.memset(zc[:], 0.0))
    prev = None
    for c in range(DCH):
        l1 = S_.dma("sync", qb[:], qkvT_dram[c * 128:(c + 1) * 128, :],
                    deps=[prev])
        l2 = S_.dma("sync", kb[:], qkvT_dram[D + c * 128:D + (c + 1) * 128, :],
                    deps=[prev])
        l3 = S_.dma("sync", vb[:], qkvT_dram[2 * D + c * 128:2 * D + (c + 1) * 128, :],
                    deps=[prev])
        l4 = S_.dma("sync", gb[:], gT_dram[c * 128:(c + 1) * 128, :],
                    deps=[prev])
        o1 = S_.op("vector", lambda h: h.tensor_tensor(kv[:], kb[:], vb[:],
                                                       ALU.mult),
                   deps=[l2, l3, prev])
        o2 = S_.op("vector", lambda h: h.tensor_tensor_scan(
            kv[:], kv[:], zc.broadcast_to([128, S]), 0.0, ALU.add, ALU.add),
            deps=[o1, z0])
        o3 = S_.op("vector", lambda h: h.tensor_tensor(kv[:], kv[:], qb[:],
                                                       ALU.mult),
                   deps=[o2, l1])
        o4 = S_.op("vector", lambda h: h.tensor_tensor(ob[:], kv[:], gb[:],
                                                       ALU.mult),
                   deps=[o3, l4])
        prev = S_.dma("gpsimd", attn_dram[c * 128:(c + 1) * 128, :], ob[:],
                      deps=[o4])
    C.pop()


def emit_glu_tok(C, cst, xqT_dram, wgqT_dram, wuqT_dram, NI, h_tok_dram):
    nc, S_ = C.nc, C.S
    C.push()
    wg = [C.psb(f"g3_wg{k}", [128, NI], BF16) for k in range(DCH)]
    wu = [C.psb(f"g3_wu{k}", [128, NI], BF16) for k in range(DCH)]
    xq = [C.psb(f"g3_xq{i}", [128, 128], BF16) for i in range(2 * DCH)]
    gbuf = [C.psb(f"g3_gb{i}", [128, TB], BF16) for i in range(2)]
    hbuf = [C.psb(f"g3_hb{i}", [128, TB], BF16) for i in range(2)]
    ps_g = [C.pps(f"g3_psg{i}", [128, TB]) for i in range(2)]
    ps_u = [C.pps(f"g3_psu{i}", [128, TB]) for i in range(2)]
    wld = []
    for k in range(DCH):
        wld.append(S_.dma("sync", wg[k][:], wgqT_dram[k * 128:(k + 1) * 128, :]))
        wld.append(S_.dma("sync", wu[k][:], wuqT_dram[k * 128:(k + 1) * 128, :]))
    nsup = ceil_div(NI, TB)
    sil_prev = [None, None]
    hb_free = [None, None]
    xld_war = {}
    for sc in range(S // 128):
        ib = sc % 2
        xlds = []
        for k in range(DCH):
            war = xld_war.get((sc - 2, k))
            xlds.append(S_.dma("sync", xq[ib * DCH + k][:],
                               xqT_dram[k * 128:(k + 1) * 128,
                                        sc * 128:(sc + 1) * 128], deps=[war]))
        for sup in range(nsup):
            ncols = min(TB, NI - sup * TB)
            pb = (sc * nsup + sup) % 2
            nsl = slice(sup * TB, sup * TB + ncols)
            mmg = mmu = None
            for k in range(DCH):
                mmg = S_.op("tensor", lambda h, k=k, ib=ib, nsl=nsl, pb=pb,
                            nc_=ncols, st=(k == 0), sp=(k == DCH - 1):
                            h.matmul(ps_g[pb][:, 0:nc_], xq[ib * DCH + k][:],
                                     wg[k][:, nsl], start=st, stop=sp),
                            deps=[xlds[k], wld[2 * k], sil_prev[pb]])
            for k in range(DCH):
                mmu = S_.op("tensor", lambda h, k=k, ib=ib, nsl=nsl, pb=pb,
                            nc_=ncols, st=(k == 0), sp=(k == DCH - 1):
                            h.matmul(ps_u[pb][:, 0:nc_], xq[ib * DCH + k][:],
                                     wu[k][:, nsl], start=st, stop=sp),
                            deps=[xlds[k], wld[2 * k + 1], sil_prev[pb]])
                xld_war[(sc, k)] = mmu
            sil = S_.op("scalar", lambda h, pb=pb, nc_=ncols: h.activation(
                gbuf[pb][:, 0:nc_], ps_g[pb][:, 0:nc_], AF.Silu),
                deps=[mmg, hb_free[pb]])
            hop = S_.op("vector", lambda h, pb=pb, nc_=ncols:
                        h.scalar_tensor_tensor(
                            hbuf[pb][:, 0:nc_], ps_u[pb][:, 0:nc_], 1.0,
                            gbuf[pb][:, 0:nc_], ALU.mult, ALU.mult),
                        deps=[mmu, sil])
            sil_prev[pb] = hop
            wr = S_.dma("gpsimd", h_tok_dram[sc * 128:(sc + 1) * 128, nsl],
                        hbuf[pb][:, 0:ncols], deps=[hop])
            hb_free[pb] = wr
    C.pop()


def emit_hq_prep(C, cst, h_tok_dram, hqT_dram, NI, halfcol):
    nc, S_ = C.nc, C.S
    C.push()
    stg = [C.psb("hq_stg", [128, NI], BF16) for _ in range(2)]
    qd = [C.psb("hq_qd", [128, NI], BF16) for _ in range(2)]
    ot = C.psb("hq_ot", [128, 128], BF16)
    scr = mk_fp4_scratch(C, NI, tag="h")
    pst = C.pps("hq_pst", [128, 128], BF16)
    prev_q = {}
    ev_prev = None
    for r in range(S // 128):
        buf = stg[r % 2]
        qb = qd[r % 2]
        ld = S_.dma("sync", buf[:], h_tok_dram[r * 128:(r + 1) * 128, :],
                    deps=[prev_q.get(r - 2)])
        qt = emit_fp4_tile(C, S_, buf[:], qb[:], 128, NI, scr, halfcol,
                           [ld, prev_q.get(r - 1), prev_q.get(r - 2)])
        prev_q[r] = qt
        for cc in range(NI // 128):
            mt = S_.op("tensor", lambda h, qb=qb, cc=cc: h.transpose(
                pst[:], qb[:, cc * 128:(cc + 1) * 128], cst["ident_bf"][:]),
                deps=[qt, ev_prev])
            ev = S_.op("vector", lambda h: h.tensor_copy(ot[:], pst[:]),
                       deps=[mt])
            ev_prev = ev
            wr = S_.dma("gpsimd", hqT_dram[cc * 128:(cc + 1) * 128,
                                           r * 128:(r + 1) * 128],
                        ot[:], deps=[ev])
            ev_prev = wr
    C.pop()


def emit_xq_prep(C, cst, xn_sb, xn_toks, xqT_dram, halfcol):
    nc, S_ = C.nc, C.S
    C.push()
    tok = C.psb("xq_tok", [128, D], BF16)
    qd = C.psb("xq_qd", [128, D], BF16)
    ot = C.psb("xq_ot", [128, 128], BF16)
    scr = mk_fp4_scratch(C, D, tag="x")
    pst = C.pps("xq_pst", [128, 128], BF16)
    ev_prev = None
    qt = None
    for r in range(S // 128):
        tts = []
        for c in range(DCH):
            mt = S_.op("tensor", lambda h, c=c, r=r: h.transpose(
                pst[:], xn_sb[c][:, r * 128:(r + 1) * 128], cst["ident_bf"][:]),
                deps=[xn_toks[c] if xn_toks else None, ev_prev])
            ev_prev = S_.op("vector", lambda h, c=c: h.tensor_copy(
                tok[:, c * 128:(c + 1) * 128], pst[:]), deps=[mt, qt])
            tts.append(ev_prev)
        qt = emit_fp4_tile(C, S_, tok[:], qd[:], 128, D, scr, halfcol, tts)
        for c in range(DCH):
            mt = S_.op("tensor", lambda h, c=c: h.transpose(
                pst[:], qd[:, c * 128:(c + 1) * 128], cst["ident_bf"][:]),
                deps=[qt, ev_prev])
            ev = S_.op("vector", lambda h: h.tensor_copy(ot[:], pst[:]),
                       deps=[mt])
            ev_prev = ev
            wr = S_.dma("gpsimd", xqT_dram[c * 128:(c + 1) * 128,
                                           r * 128:(r + 1) * 128],
                        ot[:], deps=[ev])
            ev_prev = wr
    C.pop()


def emit_layer3(C, cst, x_dram, xout_dram, w, scr):
    S_ = C.S
    halfcol = C.sb("l3_half", [128, 1], F32)
    S_.op("vector", lambda h: h.memset(halfcol[:, 0:1], 0.5))
    C.push()
    xn_sb = [C.psb(f"l3_xn{c}", [128, S], BF16) for c in range(DCH)]
    xn_toks = emit_ln(C, cst, x_dram, xn_sb)
    C.push()
    bqkv = C.psb("l3_bqkv", [128, 24], F32)
    bt = S_.dma("sync", bqkv[:], w["l3_bqkv"][:])
    emit_proj2(C, cst, xn_sb, xn_toks, w["l3_wqkvT"], 3072, scr["qkvT"],
               bqkv, bt)
    C.pop()
    C.push()
    bg = C.psb("l3_bg", [128, 8], F32)
    bt2 = S_.dma("sync", bg[:], w["l3_bgate"][:])
    emit_proj2(C, cst, xn_sb, xn_toks, w["l3_wgateT"], D, scr["gT"], bg, bt2,
               act=AF.Sigmoid)
    C.pop()
    emit_xq_prep(C, cst, xn_sb, xn_toks, scr["xqT"], halfcol[:, 0:1])
    C.pop()
    emit_attn_l3(C, cst, scr["qkvT"], scr["gT"], scr["attn"])
    emit_wq_prep(C, cst, w["l3_wgate_f"], scr["wgqT"], halfcol[:, 0:1])
    emit_wq_prep(C, cst, w["l3_wup_f"], scr["wuqT"], halfcol[:, 0:1])
    emit_wq_prep(C, cst, w["l3_wdown_f"], scr["wdqT"], halfcol[:, 0:1])
    emit_glu_tok(C, cst, scr["xqT"], scr["wgqT"], scr["wuqT"], 2816,
                 scr["h_tok"])
    emit_hq_prep(C, cst, scr["h_tok"], scr["hqT"], 2816, halfcol[:, 0:1])
    C.push()
    emit_proj_rhs(C, cst, scr["hqT"], scr["wdqT"], D, scr["ffn"],
                  wdtype=BF16)
    C.pop()
    C.push()
    emit_final_add(C, cst, x_dram, scr["attn"], scr["ffn"], xout_dram)
    C.pop()


def prep_l3(inp):
    w = {}
    w["l3_wqkvT"] = np.ascontiguousarray(inp["L3_attn_W_qkv"].T)
    w["l3_bqkv"] = colbias(inp["L3_attn_b_qkv"])
    w["l3_wgateT"] = np.ascontiguousarray(inp["L3_attn_W_gate"].T)
    w["l3_bgate"] = colbias(inp["L3_attn_b_gate"])
    w["l3_wgate_f"] = np.ascontiguousarray(inp["L3_ffn_w_gate"])
    w["l3_wup_f"] = np.ascontiguousarray(inp["L3_ffn_w_up"])
    w["l3_wdown_f"] = np.ascontiguousarray(inp["L3_ffn_w_down"])
    return w


L3_WSPECS = [
    ("l3_wqkvT", [D, 3072], F32), ("l3_bqkv", [128, 24], F32),
    ("l3_wgateT", [D, D], F32), ("l3_bgate", [128, 8], F32),
    ("l3_wgate_f", [2816, D], F32), ("l3_wup_f", [2816, D], F32),
    ("l3_wdown_f", [D, 2816], F32),
]


def declare_l3(nc):
    w = {}
    for name, shape, dt in L3_WSPECS:
        w[name] = nc.dram_tensor(name, shape, dt, kind="ExternalInput").ap()
    return w


def declare_scratch_l3(nc, tag=""):
    scr = {}
    scr["qkvT"] = nc.dram_tensor(f"s{tag}_l3qkvT", [3072, S], BF16, kind="Internal").ap()
    scr["gT"] = nc.dram_tensor(f"s{tag}_l3gT", [D, S], BF16, kind="Internal").ap()
    scr["xqT"] = nc.dram_tensor(f"s{tag}_l3xqT", [D, S], BF16, kind="Internal").ap()
    scr["wgqT"] = nc.dram_tensor(f"s{tag}_l3wgqT", [D, 2816], BF16, kind="Internal").ap()
    scr["wuqT"] = nc.dram_tensor(f"s{tag}_l3wuqT", [D, 2816], BF16, kind="Internal").ap()
    scr["wdqT"] = nc.dram_tensor(f"s{tag}_l3wdqT", [2816, D], BF16, kind="Internal").ap()
    scr["h_tok"] = nc.dram_tensor(f"s{tag}_l3htok", [S, 2816], BF16, kind="Internal").ap()
    scr["hqT"] = nc.dram_tensor(f"s{tag}_l3hqT", [2816, S], BF16, kind="Internal").ap()
    scr["attn"] = nc.dram_tensor(f"s{tag}_l3attn", [D, S], BF16, kind="Internal").ap()
    scr["ffn"] = nc.dram_tensor(f"s{tag}_l3ffn", [D, S], BF16, kind="Internal").ap()
    return scr



def l0_beta():
    j = np.arange(6, dtype=np.float64)
    alpha = (j + 1.0) ** (-1.5)
    tail = np.flip(np.cumsum(np.flip(alpha)))
    beta = np.concatenate([np.zeros(1), tail[1:], np.zeros(5)])
    beta = beta / beta.sum()
    return beta[1:6].astype(np.float32)


def emit_attn_l0(C, cst, qkvT_dram, o_dram):
    nc, S_ = C.nc, C.S
    beta = [float(b) for b in l0_beta()]
    EPSC = 1e-6
    C.push()
    qb = C.psb("a0_q", [128, S], BF16)
    kb = C.psb("a0_k", [128, S], BF16)
    vb = C.psb("a0_v", [128, S], BF16)
    qA = C.psb("a0_qA", [128, S], BF16)
    qB = C.psb("a0_qB", [128, S], BF16)
    qX = C.psb("a0_qX", [128, S], BF16)
    kA = C.psb("a0_kA", [128, S], BF16)
    kB = C.psb("a0_kB", [128, S], BF16)
    kX = C.psb("a0_kX", [128, S], BF16)
    Mn = C.psb("a0_Mn", [128, S], BF16)
    kv = C.psb("a0_kv", [128, S], BF16)
    cj = C.psb("a0_cj", [128, S], BF16)
    num = C.psb("a0_num", [128, S], BF16)
    skA = C.psb("a0_skA", [1, S], BF16)
    skB = C.psb("a0_skB", [1, S], BF16)
    sqA = C.psb("a0_sqA", [1, S], BF16)
    sqB = C.psb("a0_sqB", [1, S], BF16)
    dnA = C.psb("a0_dnA", [1, S], F32)
    dnB = C.psb("a0_dnB", [1, S], F32)
    rdb = C.psb("a0_rdb", [128, S], F32)
    zc = C.psb("a0_zc", [128, 1], F32)
    ob = C.psb("a0_ob", [128, S], BF16)
    ps1 = C.pps("a0_ps1", [1, TB])
    ps2 = C.pps("a0_ps2", [1, TB])
    psbc = C.pps("a0_psb", [128, TB])
    zt = S_.op("gpsimd", lambda h: h.memset(zc[:], 0.0))
    prev_out = None
    for c in range(DCH):
        S_.gate()
        lq = S_.dma("sync", qb[:], qkvT_dram[c * 128:(c + 1) * 128, :])
        lk = S_.dma("sync", kb[:], qkvT_dram[D + c * 128:D + (c + 1) * 128, :])
        lv = S_.dma("sync", vb[:], qkvT_dram[2 * D + c * 128:2 * D + (c + 1) * 128, :])

        def start_side(src, A, B, X, deps):
            t0 = S_.op("vector", lambda h: h.tensor_scalar(
                A[:], src[:], 0.125, 1.0 - EPSC, ALU.mult, ALU.min), deps=deps)
            t1 = S_.op("vector", lambda h: h.tensor_scalar_max(
                A[:], A[:], -1.0 + EPSC), deps=[t0])
            t2 = S_.op("scalar", lambda h: h.activation(B[:], A[:], AF.Square),
                       deps=[t1])
            t3 = S_.op("vector", lambda h: h.tensor_scalar(
                B[:], B[:], 2.0, -1.0, ALU.mult, ALU.add), deps=[t2])
            t4 = S_.op("vector", lambda h: h.tensor_scalar_mul(
                X[:], A[:], 2.0), deps=[t1])
            return t4

        tq = start_side(qb, qA, qB, qX, [lq, prev_out])
        tk = start_side(kb, kA, kB, kX, [lk, tq])
        qh = [qA, qB]
        kh = [kA, kB]
        nmz = S_.op("gpsimd", lambda h: h.memset(num[:], 0.0),
                    deps=[prev_out])
        dz1 = S_.op("gpsimd", lambda h: h.memset(dnA[:], 0.0), deps=[prev_out])
        dz2 = S_.op("gpsimd", lambda h: h.memset(dnB[:], 0.0), deps=[dz1])
        dtok = [dz1, dz2]
        ntok = nmz
        row_free = None
        for j in range(1, 6):
            if j == 1:
                Tq, Tk = qh[0], kh[0]
            elif j == 2:
                Tq, Tk = qh[1], kh[1]
            else:
                tq = S_.op("vector", lambda h, t=qh[1], X=qX: h.tensor_tensor(
                    Mn[:], t[:], X[:], ALU.mult), deps=[tq, ntok])
                tq = S_.op("vector", lambda h, o=qh[0]: h.tensor_tensor(
                    o[:], Mn[:], o[:], ALU.subtract), deps=[tq])
                qh = [qh[1], qh[0]]
                Tq = qh[1]
                tk = S_.op("vector", lambda h, t=kh[1], X=kX: h.tensor_tensor(
                    Mn[:], t[:], X[:], ALU.mult), deps=[tk, tq])
                tk = S_.op("vector", lambda h, o=kh[0]: h.tensor_tensor(
                    o[:], Mn[:], o[:], ALU.subtract), deps=[tk])
                kh = [kh[1], kh[0]]
                Tk = kh[1]
            t5 = S_.op("vector", lambda h, Tk=Tk: h.tensor_tensor(
                kv[:], Tk[:], vb[:], ALU.mult), deps=[tk, lv, tq])
            t6 = S_.op("vector", lambda h: h.tensor_tensor_scan(
                cj[:], kv[:], zc.broadcast_to([128, S]), 0.0,
                ALU.add, ALU.add), deps=[t5, zt])
            t7 = S_.op("vector", lambda h, Tq=Tq, bj=beta[j - 1]:
                       h.scalar_tensor_tensor(Mn[:], Tq[:], bj, cj[:],
                                              ALU.mult, ALU.mult),
                       deps=[t6])
            ntok = S_.op("vector", lambda h: h.tensor_tensor(
                num[:], num[:], Mn[:], ALU.add), deps=[t7, ntok])
            evs = []
            prev_e = row_free
            for which, (T, rA, rB) in enumerate(((Tk, skA, skB),
                                                 (Tq, sqA, sqB))):
                dep0 = tk if which == 0 else tq
                for t in range(NTB):
                    sl = slice(t * TB, (t + 1) * TB)
                    mmA = S_.op("tensor", lambda h, T=T, sl=sl: h.matmul(
                        ps1[:], cst["ones_col_bf"][0:64, :], T[0:64, sl],
                        start=True, stop=True), deps=[dep0, prev_e])
                    mmB = S_.op("tensor", lambda h, T=T, sl=sl: h.matmul(
                        ps2[:], cst["ones_col_bf"][64:128, :], T[64:128, sl],
                        start=True, stop=True, tile_position=(64, 0)),
                        deps=[dep0, prev_e])
                    e1 = S_.op("vector", lambda h, rA=rA, sl=sl: h.tensor_copy(
                        rA[:, sl], ps1[:]), deps=[mmA])
                    e2 = S_.op("vector", lambda h, rB=rB, sl=sl: h.tensor_copy(
                        rB[:, sl], ps2[:]), deps=[mmB, e1])
                    prev_e = e2
                    evs.append(e2)
            s1 = lp_op(S_, "vector", lambda h: h.tensor_tensor_scan(
                skA[:], skA[:], zc[0:1, :].broadcast_to([1, S]), 0.0,
                ALU.add, ALU.add), deps=evs)
            s2 = lp_op(S_, "vector", lambda h: h.tensor_tensor_scan(
                skB[:], skB[:], zc[0:1, :].broadcast_to([1, S]), 0.0,
                ALU.add, ALU.add), deps=[s1])
            d1 = S_.op("vector", lambda h, bj=beta[j - 1]:
                       h.scalar_tensor_tensor(skA[:], sqA[:], bj, skA[:],
                                              ALU.mult, ALU.mult),
                       deps=[s2] + dtok)
            d2 = S_.op("vector", lambda h: h.tensor_tensor(
                dnA[:], dnA[:], skA[:], ALU.add), deps=[d1])
            d3 = S_.op("vector", lambda h, bj=beta[j - 1]:
                       h.scalar_tensor_tensor(skB[:], sqB[:], bj, skB[:],
                                              ALU.mult, ALU.mult),
                       deps=[d2])
            d4 = S_.op("vector", lambda h: h.tensor_tensor(
                dnB[:], dnB[:], skB[:], ALU.add), deps=[d3])
            dtok = [d4]
            row_free = d4
        f1 = S_.op("gpsimd", lambda h: h.tensor_scalar_add(
            dnA[:], dnA[:], 1e-7), deps=dtok)
        f2 = S_.op("gpsimd", lambda h: h.tensor_scalar_add(
            dnB[:], dnB[:], 1e-7), deps=[f1])
        f3 = S_.op("vector", lambda h: h.reciprocal(dnA[:], dnA[:]), deps=[f2])
        f4 = S_.op("vector", lambda h: h.reciprocal(dnB[:], dnB[:]), deps=[f3])
        bts = []
        prev_b = None
        for t in range(NTB):
            sl = slice(t * TB, (t + 1) * TB)
            b1 = S_.op("tensor", lambda h, sl=sl: h.matmul(
                psbc[0:64, :], cst["ones_row_f"][:, 0:64], dnA[:, sl],
                start=True, stop=True), deps=[f4, prev_b])
            b2 = S_.op("tensor", lambda h, sl=sl: h.matmul(
                psbc[64:128, :], cst["ones_row_f"][:, 0:64], dnB[:, sl],
                start=True, stop=True, tile_position=(0, 64)),
                deps=[f4, prev_b])
            b3 = S_.op("vector", lambda h, sl=sl: h.tensor_copy(
                rdb[:, sl], psbc[:]), deps=[b1, b2])
            prev_b = b3
            bts.append(b3)
        o1 = S_.op("vector", lambda h: h.tensor_tensor(
            ob[:], num[:], rdb[:], ALU.mult), deps=bts + [ntok])
        prev_out = S_.dma("gpsimd", o_dram[c * 128:(c + 1) * 128, :], ob[:],
                          deps=[o1])
    C.pop()


def emit_xtok(C, cst, xn_sb, xn_toks, xn_tok_dram):
    nc, S_ = C.nc, C.S
    C.push()
    ot = C.psb("xt_ot", [128, D], BF16)
    pst = C.pps("xt_pst", [128, 128], BF16)
    ev_prev = None
    wr_prev = None
    for r in range(S // 128):
        evs = []
        for c in range(DCH):
            mt = S_.op("tensor", lambda h, c=c, r=r: h.transpose(
                pst[:], xn_sb[c][:, r * 128:(r + 1) * 128], cst["ident_bf"][:]),
                deps=[xn_toks[c] if xn_toks else None, ev_prev])
            ev_prev = S_.op("vector", lambda h, c=c: h.tensor_copy(
                ot[:, c * 128:(c + 1) * 128], pst[:]), deps=[mt, wr_prev])
            evs.append(ev_prev)
        wr_prev = S_.dma("gpsimd", xn_tok_dram[r * 128:(r + 1) * 128, :],
                         ot[:], deps=evs)
    C.pop()
    return wr_prev


MOE_CAP = 512
MOE_E = 8
MOE_K = 2048


def emit_moe_l0(C, cst, core_id, xn_sb, xn_toks, xn_tok_dram, w, scr):
    nc, S_ = C.nc, C.S
    NTOK = 8 * S
    FB = NTOK // 128
    OWN0 = 32 * core_id

    C.push()
    rws = [C.psb(f"mo_rw{k}", [128, MOE_E], BF16) for k in range(DCH)]
    rwstg = C.psb("mo_rwstg2", [128, MOE_E], F32)
    rb = C.psb("mo_rb2", [MOE_E, 1], F32)
    lgT = C.psb("mo_lgT2", [MOE_E, S], F32)
    sc_tok = C.psb("mo_sctok2", [128, MOE_E], F32)
    mx = C.psb("mo_mx2", [128, 1], F32)
    sm = C.psb("mo_sm2", [128, 1], F32)
    ps_l = C.pps("mo_psl2", [MOE_E, TB])
    ps_t = C.pps("mo_pst2", [128, 128])
    cvs = []
    prev = None
    for k in range(DCH):
        ld = S_.dma("sync", rwstg[:], w["l0_router_w"][k * 128:(k + 1) * 128, :],
                    deps=[prev])
        prev = S_.op("vector", lambda h, k=k: h.tensor_copy(rws[k][:], rwstg[:]),
                     deps=[ld])
        cvs.append(prev)
    lb = S_.dma("sync", rb[:], w["l0_router_b"][:])
    ev_prev = None
    for t in range(NTB):
        sl = slice(t * TB, (t + 1) * TB)
        mm = None
        for k in range(DCH):
            mm = S_.op("tensor", lambda h, k=k, sl=sl, st=(k == 0),
                       sp=(k == DCH - 1):
                       h.matmul(ps_l[:], rws[k][:], xn_sb[k][:, sl],
                                start=st, stop=sp),
                       deps=[cvs[k], xn_toks[k] if xn_toks else None, ev_prev])
        ev_prev = S_.op("scalar", lambda h, sl=sl: h.activation(
            lgT[:, sl], ps_l[:], AF.Identity, bias=rb[:, 0:1]),
            deps=[mm, lb])
    tp_prev = None
    st_prev = None
    for tc in range(S // 128):
        mt = S_.op("tensor", lambda h, tc=tc: h.transpose(
            ps_t[:, 0:MOE_E],
            lgT[:, tc * 128:(tc + 1) * 128],
            cst["ident_f"][0:MOE_E, 0:MOE_E]), deps=[ev_prev, tp_prev])
        e1 = S_.op("vector", lambda h: h.tensor_copy(
            sc_tok[:, 0:MOE_E], ps_t[0:128, 0:MOE_E]), deps=[mt, st_prev])
        tp_prev = e1
        r1 = S_.op("vector", lambda h: h.reduce_max(mx[:], sc_tok[:, 0:MOE_E],
                                                    axis=AX.X), deps=[e1])
        r2 = S_.op("gpsimd", lambda h: h.tensor_scalar_mul(mx[:], mx[:], -1.0),
                   deps=[r1])
        r3 = S_.op("scalar", lambda h: h.activation(
            sc_tok[:, 0:MOE_E], sc_tok[:, 0:MOE_E], AF.Exp, bias=mx[:, 0:1]),
            deps=[r2])
        r4 = S_.op("vector", lambda h: h.tensor_reduce(
            sm[:, 0:1], sc_tok[:, 0:MOE_E], AX.X, ALU.add), deps=[r3])
        r4b = S_.op("scalar", lambda h: h.activation(
            sm[:, 0:1], sm[:, 0:1], AF.Identity), deps=[r4])
        r5 = S_.op("vector", lambda h: h.reciprocal(sm[:, 0:1], sm[:, 0:1]),
                   deps=[r4b])
        r6 = S_.op("scalar", lambda h: h.activation(
            sc_tok[:, 0:MOE_E], sc_tok[:, 0:MOE_E], AF.Identity,
            scale=sm[:, 0:1]), deps=[r5])
        st_prev = S_.dma("gpsimd", scr["cc_in"][tc * 128:(tc + 1) * 128, :],
                         sc_tok[:, 0:MOE_E], deps=[r6])
    cc = S_.opcc("gpsimd", lambda h: h.collective_compute(
        "AllGather", ALU.bypass, [[0, 1, 2, 3, 4, 5, 6, 7]],
        [scr["cc_in"][:].opt()], [scr["cc_out"][:].opt()]), deps=[st_prev])
    C.pop()
    return cc


def emit_moe_topk(C, cst, ownsel, cc_tok, scr, posi, gsc, dbg=None):
    nc, S_ = C.nc, C.S
    NTOK = 8 * S
    FB = NTOK // 128
    BIG = 1.0e6
    C.push()
    sc_f = C.psb("mk_scf", [128, FB * 8], F32)
    cmp8 = C.psb("mk_cmp", [128, 8 * FB], BF16)
    cntpe = C.psb("mk_cntpe", [128, 8], BF16)
    crow = C.psb("mk_crow", [1, 8], F32)
    c128 = C.psb("mk_c128", [128, 8], F32)
    sel = C.psb("mk_sel", [128, 8], F32)
    mscr = C.psb("mk_mscr", [128, 8], F32)
    lo = C.psb("mk_lo", [128, 8], F32)
    hi = C.psb("mk_hi", [128, 8], F32)
    mid = C.psb("mk_mid", [128, 8], F32)
    need = C.psb("mk_need", [128, 8], F32)
    sgt = C.psb("mk_sgt", [128, FB * 8], BF16)
    band = C.psb("mk_band", [128, FB * 8], BF16)
    scT = C.psb("mk_scT", [128, FB * 8], F32)
    gtT = C.psb("mk_gtT", [128, FB * 8], BF16)
    bdT = C.psb("mk_bdT", [128, FB * 8], BF16)
    ppf = C.psb("mk_ppf", [128, FB * 8], F32)
    colx = C.psb("mk_colx", [1, FB * 8], F32)
    colxb = C.psb("mk_colxb", [128, FB * 8], F32)
    fsb = C.psb("mk_fsb", [128, 8], F32)
    ps_c = C.pps("mk_psc", [1, 8])
    ps_b8 = C.pps("mk_psb8", [128, 8])
    ps_tb = C.pps("mk_pstb", [128, 128], BF16)
    ps_tf = C.pps("mk_pstf", [128, 128])
    ohsb = C.psb("mk_ohsb", [1, 256], F32)
    selsb = C.psb("mk_selsb", [128, 16 * 256], F32)
    oh_ld = S_.dma("sync", ohsb[:], ownsel["ownhot_d"][:])
    sel_ld = S_.dma("sync", selsb.rearrange("p (b c) -> p b c", b=16),
                    ownsel["sel_d"][:])
    ownsel = {"ownhot": ohsb, "sel": selsb.rearrange("p (b c) -> p b c", b=16),
              "ld": [oh_ld, sel_ld]}
    scr["ps4"] = [C.pps("mk_ps4a", [128, 2048])]
    crow2 = C.psb("mk_crow2", [1, 2048], F32)
    ld = S_.dma("sync", sc_f.rearrange("p (f e) -> p f e", e=8),
                scr["cc_out"].rearrange("(p f) e -> p f e", p=128),
                deps=[cc_tok])
    i1 = S_.op("gpsimd", lambda h: h.memset(lo[:], 0.0))
    i2 = S_.op("gpsimd", lambda h: h.memset(hi[:], 1.0))
    upd = [i1, i2, ld]

    cg_state = {"prev": None}

    def count_gt(thr, deps):
        c1 = S_.op("vector", lambda h, thr=thr: h.tensor_tensor(
            cmp8.rearrange("p (f e) -> p f e", e=8),
            sc_f.rearrange("p (f e) -> p f e", e=8),
            thr[:].unsqueeze(1).broadcast_to([128, FB, 8]), ALU.is_gt),
            deps=deps)
        mms = []
        for q in range(4):
            qsl = slice(q * TB, (q + 1) * TB)
            mms.append(S_.op("tensor", lambda h, qsl=qsl: h.matmul(
                scr["ps4"][0][0:1, qsl], cst["ones_col_bf"][:],
                cmp8[:, qsl], start=True, stop=True),
                deps=[c1, cg_state["prev"]]))
        c2 = S_.op("vector", lambda h: h.tensor_copy(crow2[:],
                                                     scr["ps4"][0][0:1, :]),
                   deps=mms)
        c3 = c2
        for e in range(8):
            c3 = S_.op("vector", lambda h, e=e: h.tensor_reduce(
                crow[:, e:e + 1].unsqueeze(2),
                crow2.rearrange("a (f ee) -> a ee f", ee=8)[:, e:e + 1, :],
                AX.X, ALU.add), deps=[c3])
        c3b = S_.op("scalar", lambda h: h.activation(
            crow[:], crow[:], AF.Identity), deps=[c3])
        c5 = S_.op("tensor", lambda h: h.matmul(
            ps_b8[:], cst["ones_row_f"][:], crow[:], start=True, stop=True),
            deps=[c3b])
        c6 = S_.op("vector", lambda h: h.tensor_copy(c128[:], ps_b8[:]),
                   deps=[c5])
        cg_state["prev"] = c2
        return c6

    for it in range(30):
        m1 = S_.op("vector", lambda h: h.tensor_tensor(
            mid[:], lo[:], hi[:], ALU.add), deps=upd)
        m2 = S_.op("vector", lambda h: h.tensor_scalar_mul(mid[:], mid[:], 0.5),
                   deps=[m1])
        c6 = count_gt(mid, [m2])
        if it == 0 and dbg is not None and "cnt0" in dbg:
            S_.dma("gpsimd", dbg["cnt0"], c128[:], deps=[c6])
            S_.dma("gpsimd", dbg["cntpe0"], cntpe[:], deps=[c6])
            S_.dma("gpsimd", dbg["mid0"], mid[:], deps=[c6])
            c6 = S_.op("scalar", lambda h: h.activation(
                c128[:], c128[:], AF.Identity),
                deps=[c6, Tok("d_gpsimd_0", 0, None, False)])
            S_.gate()
        s1 = S_.op("vector", lambda h: h.tensor_single_scalar(
            sel[:], c128[:], float(MOE_K), ALU.is_ge), deps=[c6])
        u1 = S_.op("vector", lambda h: h.tensor_tensor(
            mscr[:], mid[:], lo[:], ALU.subtract), deps=[s1])
        u2 = S_.op("vector", lambda h: h.tensor_tensor(
            mscr[:], mscr[:], sel[:], ALU.mult), deps=[u1])
        u3 = S_.op("vector", lambda h: h.tensor_tensor(
            lo[:], lo[:], mscr[:], ALU.add), deps=[u2])
        u4 = S_.op("vector", lambda h: h.tensor_tensor(
            mscr[:], hi[:], mid[:], ALU.subtract), deps=[u3])
        u5 = S_.op("vector", lambda h: h.tensor_tensor(
            mscr[:], mscr[:], sel[:], ALU.mult), deps=[u4])
        u6 = S_.op("vector", lambda h: h.tensor_tensor(
            hi[:], mid[:], mscr[:], ALU.add), deps=[u5])
        u7 = S_.op("scalar", lambda h: h.activation(lo[:], lo[:], AF.Identity),
                   deps=[u6])
        u8 = S_.op("scalar", lambda h: h.activation(hi[:], hi[:], AF.Identity),
                   deps=[u7])
        upd = [u8]
    c6 = count_gt(hi, upd)
    n1 = S_.op("vector", lambda h: h.tensor_scalar(
        need[:], c128[:], -1.0, float(MOE_K), ALU.mult, ALU.add), deps=[c6])
    g1 = S_.op("vector", lambda h: h.tensor_tensor(
        sgt.rearrange("p (f e) -> p f e", e=8),
        sc_f.rearrange("p (f e) -> p f e", e=8),
        hi[:].unsqueeze(1).broadcast_to([128, FB, 8]), ALU.is_gt), deps=[n1])
    g2 = S_.op("vector", lambda h: h.tensor_tensor(
        band.rearrange("p (f e) -> p f e", e=8),
        sc_f.rearrange("p (f e) -> p f e", e=8),
        lo[:].unsqueeze(1).broadcast_to([128, FB, 8]), ALU.is_gt), deps=[g1])
    g3 = S_.op("vector", lambda h: h.tensor_tensor(
        band[:], band[:], sgt[:], ALU.subtract), deps=[g2])
    tts = []
    tprev = None
    for which, (srcT, dstT, idt, pst, dt) in enumerate((
            (sgt, gtT, "ident_bf", ps_tb, BF16),
            (band, bdT, "ident_bf", ps_tb, BF16),
            (sc_f, scT, "ident_f", ps_tf, F32))):
        for e in range(8):
            for g in range(2):
                mt = S_.op("tensor", lambda h, srcT=srcT, idt=idt, pst=pst,
                           e=e, g=g:
                           h.transpose(pst[:],
                                       srcT.rearrange("p (f ee) -> p f ee",
                                                      ee=8)[:, g * 128:(g + 1) * 128, e],
                                       cst[idt][:]),
                           deps=[g3, tprev])
                ev = S_.op("vector", lambda h, dstT=dstT, pst=pst, e=e, g=g:
                           h.tensor_copy(
                               dstT.rearrange("p (f ee) -> p f ee", ee=8)
                               [:, g:FB:2, e], pst[:]),
                           deps=[mt])
                tprev = ev
                tts.append(ev)

    def prefix_round(maskT, deps):
        mms = []
        p1 = None
        for q in range(4):
            qsl = slice(q * TB, (q + 1) * TB)
            mm = S_.op("tensor", lambda h, qsl=qsl: h.matmul(
                scr["ps4"][0][:, qsl], cst["ustrict_bf"][:], maskT[:, qsl],
                start=True, stop=True), deps=deps + [p1])
            mms.append(mm)
        e1 = S_.op("vector", lambda h: h.tensor_copy(ppf[:], scr["ps4"][0][:]),
                   deps=mms)
        cms = []
        for q in range(4):
            qsl = slice(q * TB, (q + 1) * TB)
            cm = S_.op("tensor", lambda h, qsl=qsl: h.matmul(
                scr["ps4"][0][0:1, qsl], cst["ones_col_bf"][:], maskT[:, qsl],
                start=True, stop=True), deps=[e1])
            cms.append(cm)
        e2 = S_.op("vector", lambda h: h.tensor_copy(colx[:],
                                                     scr["ps4"][0][0:1, :]),
                   deps=cms)
        sc_tok = e2
        for e in range(8):
            sc_tok = S_.op("vector", lambda h, e=e: h.tensor_tensor_scan(
                colx.rearrange("a (f ee) -> a f ee", ee=8)[:, :, e],
                colx.rearrange("a (f ee) -> a f ee", ee=8)[:, :, e],
                cst["zero_col"][0:1, 0:1].broadcast_to([1, FB]), 0.0,
                ALU.add, ALU.add), deps=[sc_tok])
        sb1 = S_.op("scalar", lambda h: h.activation(
            colx[:], colx[:], AF.Identity), deps=[sc_tok])
        sh = S_.op("vector", lambda h: h.tensor_copy(
            colxb[0:1, 8:FB * 8], colx[0:1, 0:(FB - 1) * 8]), deps=[sb1])
        sh2 = S_.op("vector", lambda h: h.memset(colxb[0:1, 0:8], 0.0),
                    deps=[sh])
        bts = []
        bprev = sh2
        for q in range(4):
            qsl = slice(q * TB, (q + 1) * TB)
            bm = S_.op("tensor", lambda h, qsl=qsl: h.matmul(
                scr["ps4"][0][:, qsl], cst["ones_row_f"][:],
                colxb[0:1, qsl], start=True, stop=True), deps=[bprev, e1])
            bts.append(bm)
        e3 = S_.op("vector", lambda h: h.tensor_copy(colxb[:],
                                                     scr["ps4"][0][:]),
                   deps=bts)
        f1 = S_.op("vector", lambda h: h.tensor_tensor(
            ppf[:], ppf[:], colxb[:], ALU.add), deps=[e3])
        return f1

    f1 = prefix_round(bdT, tts)
    x2 = S_.op("vector", lambda h: h.tensor_tensor(
        band.rearrange("p (f e) -> p f e", e=8),
        ppf.rearrange("p (f e) -> p f e", e=8),
        need[:].unsqueeze(1).broadcast_to([128, FB, 8]), ALU.is_lt),
        deps=[f1])
    x3 = S_.op("vector", lambda h: h.tensor_tensor(
        band[:], band[:], bdT[:], ALU.mult), deps=[x2])
    x4 = S_.op("vector", lambda h: h.tensor_tensor(
        gtT[:], gtT[:], band[:], ALU.add), deps=[x3])
    f2 = prefix_round(gtT, [x4])
    fr1 = S_.op("vector", lambda h: h.tensor_tensor(
        band[0:1, :].rearrange("p (e f) -> p e f", e=8),
        colxb[0:1, :].rearrange("p (f e) -> p e f", e=8),
        ownsel["ownhot"][0:1, :].unsqueeze(1).broadcast_to([1, 8, FB]),
        ALU.mult), deps=[f2] + ownsel["ld"])
    fr2 = S_.op("vector", lambda h: h.tensor_reduce(
        crow[:].unsqueeze(2), band[0:1, 0:8 * FB].rearrange(
            "p (e f) -> p e f", e=8), AX.X, ALU.add), deps=[fr1])
    fr2b = S_.op("scalar", lambda h: h.activation(
        crow[:], crow[:], AF.Identity), deps=[fr2])
    fr3 = S_.op("tensor", lambda h: h.matmul(
        ps_b8[:], cst["ones_row_f"][:], crow[:], start=True, stop=True),
        deps=[fr2b])
    fr4 = S_.op("vector", lambda h: h.tensor_copy(fsb[:], ps_b8[:]),
                deps=[fr3])
    p1 = S_.op("vector", lambda h: h.tensor_tensor(
        ppf.rearrange("p (f e) -> p f e", e=8),
        ppf.rearrange("p (f e) -> p f e", e=8),
        fsb[:].unsqueeze(1).broadcast_to([128, FB, 8]), ALU.subtract),
        deps=[fr4])
    p2 = S_.op("vector", lambda h: h.tensor_tensor(
        ppf[:], ppf[:], gtT[:], ALU.mult), deps=[p1])
    p3 = S_.op("vector", lambda h: h.tensor_scalar(
        band[:], gtT[:], -BIG, BIG, ALU.mult, ALU.add), deps=[p2])
    p4 = S_.op("vector", lambda h: h.tensor_tensor(
        ppf[:], ppf[:], band[:], ALU.add), deps=[p3])
    selmm_prev = None
    for which, (srcT, dst, cast_int) in enumerate(((ppf, posi, True),
                                                   (scT, gsc, False))):
        mms = []
        tp = None
        for b in range(16):
            mt = S_.op("tensor", lambda h, srcT=srcT, b=b: h.transpose(
                ps_tf[:], srcT[:, b * 128:(b + 1) * 128], cst["ident_f"][:]),
                deps=[p4, tp, selmm_prev])
            ev = S_.op("vector", lambda h: h.tensor_copy(colxb[:, 0:128],
                                                         ps_tf[:]),
                       deps=[mt])
            tp = ev
            mm = S_.op("tensor", lambda h, b=b, st=(b == 0), sp=(b == 15):
                       h.matmul(scr["ps4"][0][:, 0:256], colxb[:, 0:128],
                                ownsel["sel"][:, b, :], start=st, stop=sp),
                       deps=[ev, selmm_prev] + ownsel["ld"])
            tp = mm
            mms.append(mm)
        if cast_int:
            selmm_prev = S_.op("vector", lambda h, dst=dst:
                               h.tensor_scalar_min(dst[:],
                                                   scr["ps4"][0][:, 0:256],
                                                   float(MOE_CAP)),
                               deps=[mms[-1]])
        else:
            selmm_prev = S_.op("vector", lambda h, dst=dst: h.tensor_copy(
                dst[:], scr["ps4"][0][:, 0:256]), deps=[mms[-1]])
    q2 = selmm_prev
    if dbg is not None:
        S_.dma("gpsimd", dbg["lo"], lo[:], deps=[q2])
        S_.dma("gpsimd", dbg["hi"], hi[:], deps=[q2])
        S_.dma("gpsimd", dbg["need"], need[:], deps=[q2])
        S_.dma("gpsimd", dbg["frank"], ppf[:], deps=[q2])
        S_.dma("gpsimd", dbg["colx"], colxb[0:1, :], deps=[q2])
    C.pop()
    return q2


def emit_moe_ffn(C, cst, xn_tok_dram, xtok_tok, topk_tok, posi, gsc, w, scr,
                 dbg=None):
    nc, S_ = C.nc, C.S
    C.push()
    zb = C.psb("mf_zb", [128, D], BF16)
    zf = C.psb("mf_zf", [128, D], F32)
    zi = C.psb("mf_zi", [128, 8], I32)
    z1 = S_.op("vector", lambda h: h.memset(zb[:], 0.0))
    z2 = S_.op("vector", lambda h: h.memset(zf[:], 0.0))
    z3 = S_.op("vector", lambda h: h.memset(zi[:], 100000))
    fills = []
    t = None
    for r in range(S // 128):
        t = S_.dma("gpsimd", scr["y_moe"][r * 128:(r + 1) * 128, :], zf[:],
                   deps=[z2])
        fills.append(t)
    for e in range(8):
        for q in range(MOE_CAP // 128):
            fills.append(S_.dma("gpsimd",
                                scr["gath"][e][q * 128:(q + 1) * 128, :],
                                zb[:], deps=[z1]))
            fills.append(S_.dma("gpsimd",
                                scr["gid"][e][q * 128:(q + 1) * 128, :],
                                zi[:], deps=[z3]))
            fills.append(S_.dma("gpsimd",
                                scr["gsc"][e][q * 128:(q + 1) * 128, :],
                                zf[:, 0:8], deps=[z2]))
    xchunk = C.psb("mf_xch", [128, D], BF16)
    gscr = C.psb("mf_gscr", [128, 2048], F32)
    grep = S_.op("vector", lambda h: h.tensor_copy(
        gscr.rearrange("p (c r) -> p c r", r=8),
        gsc[:].unsqueeze(2).broadcast_to([128, 256, 8])), deps=[topk_tok])
    sc_prev = fills + [grep]
    last_sc = None
    for tc in range(32):
        lx = S_.dma("sync", xchunk[:], xn_tok_dram[tc * 128:(tc + 1) * 128, :],
                    deps=[xtok_tok, last_sc])
        for e in range(8):
            col = posi[:, tc * 8 + e:tc * 8 + e + 1]
            s1 = S_.opd("gpsimd", lambda h, e=e, col=col: h.indirect_dma_start(
                out=scr["gath"][e][:],
                out_offset=IndirectOffsetOnAxis(col, 0),
                in_=xchunk[:], in_offset=None), deps=[lx, topk_tok] + sc_prev)
            s2 = S_.opd("gpsimd", lambda h, e=e, tc=tc, col=col:
                       h.indirect_dma_start(
                           out=scr["gid"][e][:],
                           out_offset=IndirectOffsetOnAxis(col, 0),
                           in_=cst["moe_iota"][:, tc * 8:(tc + 1) * 8],
                           in_offset=None), deps=[s1])
            s3 = S_.opd("gpsimd", lambda h, e=e, tc=tc, col=col:
                       h.indirect_dma_start(
                           out=scr["gsc"][e][:],
                           out_offset=IndirectOffsetOnAxis(col, 0),
                           in_=gscr[:, (tc * 8 + e) * 8:(tc * 8 + e + 1) * 8],
                           in_offset=None), deps=[s2])
            sc_prev = [s3]
            last_sc = s3
    scat_done = last_sc
    C.pop()

    for e in range(8):
        C.push()
        gstg = C.psb("mf_gstg", [128, D], BF16)
        gT = [C.psb(f"mf_gT{k}", [128, MOE_CAP], BF16) for k in range(DCH)]
        w1stg = [C.psb(f"mf_w1s{i}", [128, 1024], F32) for i in range(2)]
        w1b = [C.psb(f"mf_w1b{k}", [128, 1024], BF16) for k in range(DCH)]
        h1T = [C.psb(f"mf_h1T{f}", [128, MOE_CAP], BF16) for f in range(32)]
        b1s = C.psb("mf_b1s", [128, 32], F32)
        C.push()
        ps_h = [C.pps(f"mf_psh{i}", [128, MOE_CAP]) for i in range(2)]
        ps_t = C.pps("mf_pst", [128, 128], BF16)
        ps_g = C.pps("mf_psg", [128, MOE_CAP])
        tprev = None
        eprev = None
        for q in range(MOE_CAP // 128):
            lg = S_.dma("sync", gstg[:],
                        scr["gath"][e][q * 128:(q + 1) * 128, :],
                        deps=[scat_done, tprev])
            for c in range(DCH):
                mt = S_.op("tensor", lambda h, c=c: h.transpose(
                    ps_t[:], gstg[:, c * 128:(c + 1) * 128],
                    cst["ident_bf"][:]), deps=[lg, eprev])
                eprev = S_.op("vector", lambda h, c=c, q=q: h.tensor_copy(
                    gT[c][:, q * 128:(q + 1) * 128], ps_t[:]), deps=[mt])
            tprev = eprev
        lb1 = S_.dma("sync", b1s[:], w["l0_b1"][:, e * 32:(e + 1) * 32])
        mm_tail = {}
        cast_prev = {}
        ev_h = [None] * 32
        for sup in range(4):
            ib = sup % 2
            for k in range(DCH):
                stg_war = (cast_prev.get((sup, k - 1)) if k
                           else cast_prev.get((sup - 2, DCH - 1)))
                ldw = S_.dma("sync", w1stg[ib][:],
                             w["l0_w1"][e, k * 128:(k + 1) * 128,
                                        sup * 1024:(sup + 1) * 1024],
                             deps=[stg_war, mm_tail.get(sup - 1)])
                cast_prev[(sup, k)] = S_.op(
                    "vector", lambda h, k=k, ib=ib:
                    h.tensor_copy(w1b[k][:] if False else w1b[k][:],
                                  w1stg[ib][:]), deps=[ldw])
            for fc in range(8):
                fabs = sup * 8 + fc
                pb = fabs % 2
                mm = None
                for k in range(DCH):
                    mm = S_.op("tensor", lambda h, k=k, fc=fc, pb=pb,
                               st=(k == 0), sp=(k == DCH - 1):
                               h.matmul(ps_h[pb][:],
                                        w1b[k][:, fc * 128:(fc + 1) * 128],
                                        gT[k][:], start=st, stop=sp),
                               deps=[cast_prev[(sup, k)], tprev,
                                     ev_h[fabs - 2] if fabs >= 2 else None])
                ev_h[fabs] = S_.op("scalar", lambda h, fabs=fabs, pb=pb:
                                   h.activation(h1T[fabs][:], ps_h[pb][:],
                                                AF.Relu,
                                                bias=b1s[:, fabs:fabs + 1]),
                                   deps=[mm, lb1])
                mm_tail[sup] = mm
        if e == 0 and dbg is not None:
            for ff in range(32):
                S_.dma("gpsimd", dbg["h1full"][ff * 128:(ff + 1) * 128, :],
                       h1T[ff][:], deps=[ev_h[31]])
            S_.dma("gpsimd", dbg["gT0"], gT[0][:], deps=[ev_h[31]])
        C.pop()
        w2stg = [C.psb(f"mf_w2s{i}", [128, 1024], F32) for i in range(2)]
        w2b = [C.psb(f"mf_w2b{i}", [128, 1024], BF16) for i in range(2)]
        b2row = C.psb("mf_b2row", [1, 1024], BF16)
        b2rf = C.psb("mf_b2rf", [1, 1024], F32)
        gcols = C.psb("mf_gcols", [128, 4], F32)
        ctokf = C.psb("mf_ctokf", [128, D], F32)
        gidc = C.psb("mf_gidc", [128, 4], I32)
        ps_eo = [C.pps(f"mf_pseo{i}", [128, 1024]) for i in range(4)]
        lb2r = S_.dma("sync", b2rf[:], w["l0_b2r"][e:e + 1, :])
        cb2 = S_.op("vector", lambda h: h.tensor_copy(b2row[:], b2rf[:]),
                    deps=[lb2r])
        lgc = S_.dma("sync", gcols[:],
                     scr["gsc"][e][0:MOE_CAP, 0:1].rearrange(
                         "(q p) b -> p (q b)", p=128),
                     deps=[scat_done], allow_slow_non_contiguous=True)
        cast2 = {}
        mm_last = {}
        for fk in range(32):
            ib2 = fk % 2
            stg_war = cast2.get(fk - 2)
            ldw = S_.dma("sync", w2stg[ib2][:],
                         w["l0_w2"][e, fk * 128:(fk + 1) * 128, :],
                         deps=[stg_war, mm_last.get(fk - 2)])
            cv = S_.op("vector", lambda h, ib2=ib2: h.tensor_copy(
                w2b[ib2][:], w2stg[ib2][:]), deps=[ldw])
            cast2[fk] = cv
            for sc4 in range(4):
                for hf in range(2):
                    mm = S_.op("tensor", lambda h, fk=fk, ib2=ib2, sc4=sc4,
                               hf=hf, st=(fk == 0):
                               h.matmul(ps_eo[sc4][:, hf * 512:(hf + 1) * 512],
                                        h1T[fk][:, sc4 * 128:(sc4 + 1) * 128],
                                        w2b[ib2][:, hf * 512:(hf + 1) * 512],
                                        start=st, stop=False),
                               deps=[cv, ev_h[fk]])
            mm_last[fk] = mm
        bm = [None] * 4
        for sc4 in range(4):
            for hf in range(2):
                bm[sc4] = S_.op("tensor", lambda h, sc4=sc4, hf=hf:
                                h.matmul(ps_eo[sc4][:, hf * 512:(hf + 1) * 512],
                                         cst["ones_row_bf"][:],
                                         b2row[:, hf * 512:(hf + 1) * 512],
                                         start=False, stop=True),
                                deps=[mm_last[31], cb2])
        sc_add_prev = None
        for sc4 in range(4):
            ev = S_.op("scalar", lambda h, sc4=sc4: h.activation(
                ctokf[:], ps_eo[sc4][:], AF.Identity,
                scale=gcols[:, sc4:sc4 + 1]),
                deps=[bm[sc4], lgc, sc_add_prev])
            lgi = S_.dma("sync", gidc[:, sc4:sc4 + 1],
                         scr["gid"][e][sc4 * 128:(sc4 + 1) * 128, 0:1],
                         deps=[scat_done], allow_slow_non_contiguous=True)
            lgi2 = S_.op("vector", lambda h, sc4=sc4: h.tensor_scalar_min(
                gidc[:, sc4:sc4 + 1], gidc[:, sc4:sc4 + 1], float(S)),
                deps=[lgi])
            lgi3 = S_.op("scalar", lambda h, sc4=sc4: h.activation(
                gidc[:, sc4:sc4 + 1], gidc[:, sc4:sc4 + 1], AF.Identity),
                deps=[lgi2])
            sc_add_prev = S_.opd(
                "gpsimd", lambda h, sc4=sc4: h.indirect_dma_start(
                    out=scr["y_moe"][:],
                    out_offset=IndirectOffsetOnAxis(gidc[:, sc4:sc4 + 1], 0),
                    in_=ctokf[:], in_offset=None,
                    compute_op=ALU.add),
                deps=[ev, lgi3, sc_add_prev] + fills)
        C.pop()
    return sc_add_prev


def emit_ymoe_T(C, cst, y_moe_dram, ffn_dram, dep):
    nc, S_ = C.nc, C.S
    C.push()
    stg = [C.psb("yt_stg", [128, D], F32) for _ in range(2)]
    bfb = C.psb("yt_bfb", [128, D], BF16)
    ot = C.psb("yt_ot", [128, 128 * DCH], F32)
    pst = C.pps("yt_pst", [128, 128], BF16)
    ev_prev = None
    wr_prev = {}
    for r in range(S // 128):
        buf = stg[r % 2]
        ld = S_.dma("sync", buf[:], y_moe_dram[r * 128:(r + 1) * 128, :],
                    deps=[dep, wr_prev.get(r - 2)])
        wrs = []
        cb = S_.op("vector", lambda h, buf=buf: h.tensor_copy(
            bfb[:], buf[:]), deps=[ld, ev_prev])
        for c in range(DCH):
            mt = S_.op("tensor", lambda h, c=c: h.transpose(
                pst[:], bfb[:, c * 128:(c + 1) * 128], cst["ident_bf"][:]),
                deps=[cb, ev_prev])
            ev_prev = S_.op("vector", lambda h, c=c: h.tensor_copy(
                ot[:, c * 128:(c + 1) * 128], pst[:]),
                deps=[mt, wr_prev.get(r - 1)])
            wrs.append(S_.dma("gpsimd",
                              ffn_dram[c * 128:(c + 1) * 128,
                                       r * 128:(r + 1) * 128],
                              ot[:, c * 128:(c + 1) * 128], deps=[ev_prev]))
        wr_prev[r] = wrs[-1]
    C.pop()


def emit_layer0(C, cst, core_id, x_dram, xout_dram, w, scr):
    S_ = C.S
    C.push()
    xn_sb = [C.psb(f"l0_xn{c}", [128, S], BF16) for c in range(DCH)]
    xn_toks = emit_ln(C, cst, x_dram, xn_sb)
    xtok_tok = emit_xtok(C, cst, xn_sb, xn_toks, scr["xn_tok"])
    C.push()
    emit_proj2(C, cst, xn_sb, xn_toks, w["l0_winT"], 3072, scr["qkvT"],
               None, None)
    C.pop()
    cc = emit_moe_l0(C, cst, core_id, xn_sb, xn_toks, scr["xn_tok"], w, scr)
    C.pop()
    posi = C.sb("l0_posi", [128, 256], I32)
    gsc = C.sb("l0_gsc", [128, 256], F32)
    ownsel = {"ownhot_d": w["l0_ownhot"], "sel_d": w["l0_sel"]}
    tk = emit_moe_topk(C, cst, ownsel, cc, scr, posi, gsc,
                       dbg=scr.get("dbg_topk"))
    emit_attn_l0(C, cst, scr["qkvT"], scr["o"])
    C.push()
    emit_proj_rhs(C, cst, scr["o"], w["l0_woutT"], D, scr["attn"])
    C.pop()
    sc_done = emit_moe_ffn(C, cst, scr["xn_tok"], xtok_tok, tk, posi, gsc,
                           w, scr, dbg=scr.get("dbg_ffn2"))
    emit_ymoe_T(C, cst, scr["y_moe"], scr["ffn_f32"], sc_done)
    C.push()
    emit_final_add(C, cst, x_dram, scr["attn"], scr["ffn_f32"], xout_dram,
                   ffn_f32=True)
    C.pop()


def prep_l0_core(r):
    w = {}
    oh = np.zeros((1, 256), np.float32)
    oh[0, 32 * r] = 1.0
    w["l0_ownhot"] = oh
    sel = np.zeros((16, 128, 256), np.float32)
    for tc in range(32):
        for e in range(8):
            flat = (32 * r + tc) * 8 + e
            sel[flat // 128, flat % 128, tc * 8 + e] = 1.0
    w["l0_sel"] = np.ascontiguousarray(sel.transpose(1, 0, 2))
    return w


def prep_l0(inp):
    w = {}
    w["l0_winT"] = np.ascontiguousarray(inp["L0_attn_W_in"].T)
    w["l0_woutT"] = np.ascontiguousarray(inp["L0_attn_W_out"].T)
    w["l0_router_w"] = np.ascontiguousarray(inp["L0_ffn_router_w"])
    w["l0_router_b"] = np.ascontiguousarray(
        inp["L0_ffn_router_b"].reshape(8, 1))
    w["l0_w1"] = np.ascontiguousarray(inp["L0_ffn_w1"])
    w["l0_w2"] = np.ascontiguousarray(inp["L0_ffn_w2"])
    w["l0_b1"] = np.concatenate([colbias(inp["L0_ffn_b1"][e])
                                 for e in range(8)], axis=1)
    w["l0_b2"] = np.concatenate([colbias(inp["L0_ffn_b2"][e])
                                 for e in range(8)], axis=1)
    w["l0_b2r"] = np.ascontiguousarray(inp["L0_ffn_b2"])
    return w


L0_WSPECS = [
    ("l0_winT", [D, 3072], F32), ("l0_woutT", [D, D], F32),
    ("l0_router_w", [D, 8], F32), ("l0_router_b", [8, 1], F32),
    ("l0_w1", [8, D, 4096], F32), ("l0_w2", [8, 4096, D], F32),
    ("l0_b1", [128, 256], F32), ("l0_b2", [128, 64], F32),
    ("l0_b2r", [8, 1024], F32),
    ("l0_ownhot", [1, 256], F32), ("l0_sel", [128, 16, 256], F32),
]


def declare_l0(nc):
    w = {}
    for name, shape, dt in L0_WSPECS:
        w[name] = nc.dram_tensor(name, shape, dt, kind="ExternalInput").ap()
    return w


def declare_scratch_l0(nc, tag=""):
    scr = {}
    scr["qkvT"] = nc.dram_tensor(f"s{tag}_l0qkvT", [3072, S], BF16, kind="Internal").ap()
    scr["xn_tok"] = nc.dram_tensor(f"s{tag}_l0xtok", [S, D], BF16, kind="Internal").ap()
    scr["o"] = nc.dram_tensor(f"s{tag}_l0o", [D, S], BF16, kind="Internal").ap()
    scr["attn"] = nc.dram_tensor(f"s{tag}_l0attn", [D, S], BF16, kind="Internal").ap()
    scr["ffn_f32"] = nc.dram_tensor(f"s{tag}_l0ffn", [D, S], F32, kind="Internal").ap()
    scr["cc_in"] = nc.dram_tensor(f"s{tag}_l0ccin", [S, 8], F32, kind="Internal").ap()
    scr["cc_out"] = nc.dram_tensor(f"s{tag}_l0ccout", [8 * S, 8], F32, kind="Internal").ap()
    scr["y_moe"] = nc.dram_tensor(f"s{tag}_l0ymoe", [S + 128, D], F32, kind="Internal").ap()
    scr["gath"] = [nc.dram_tensor(f"s{tag}_l0gath{e}", [MOE_CAP + 128, D], BF16, kind="Internal").ap() for e in range(8)]
    scr["gid"] = [nc.dram_tensor(f"s{tag}_l0gid{e}", [MOE_CAP + 128, 8], I32, kind="Internal").ap() for e in range(8)]
    scr["gsc"] = [nc.dram_tensor(f"s{tag}_l0gsc{e}", [MOE_CAP + 128, 8], F32, kind="Internal").ap() for e in range(8)]
    return scr


DT_MAP = {"bfloat16": BF16, "float32": F32, "int32": I32}


def _build_layer(idx):
    nc = bass.Bass("TRN2", target_bir_lowering=False, debug=False)
    x_in = nc.dram_tensor("x_in", [D, S], F32, kind="ExternalInput").ap()
    x_out = nc.dram_tensor("x_out", [D, S], F32, kind="ExternalOutput").ap()
    decl = [declare_l0, declare_l1, declare_l2, declare_l3][idx]
    w = decl(nc)
    cst_dram = {}
    for name, arr in make_host_consts().items():
        cst_dram[name] = nc.dram_tensor(name, list(arr.shape),
                                        DT_MAP[arr.dtype.name],
                                        kind="ExternalInput").ap()
    S_ = Sched(nc)
    C = Ctx(nc, S_)
    cst = load_consts(C, cst_dram)
    if idx == 0:
        scr = declare_scratch_l0(nc)
        emit_layer0(C, cst, 0, x_in, x_out, w, scr)
    elif idx == 1:
        scr = declare_scratch_l1(nc)
        emit_layer1(C, cst, x_in, x_out, w, scr)
    elif idx == 2:
        scr = declare_scratch_l2(nc)
        emit_layer2(C, cst, x_in, x_out, w, scr)
    else:
        scr = declare_scratch_l3(nc)
        emit_layer3(C, cst, x_in, x_out, w, scr)
    finish(C)
    S_.emit()
    return nc


def kernel(**inputs):
    consts = make_host_consts()
    preps = [prep_l0(inputs), prep_l1(inputs), prep_l2(inputs),
             prep_l3(inputs)]
    x = inputs["L0_attn_x"]
    xs = [np.ascontiguousarray(x[b].T.astype(np.float32)) for b in range(8)]
    for li in range(4):
        nc = _build_layer(li)
        in_maps = []
        for b in range(8):
            m = {"x_in": xs[b]}
            m.update(preps[li])
            if li == 0:
                m.update(prep_l0_core(b))
            m.update(consts)
            in_maps.append(m)
        res = run_bass_kernel_spmd(nc, in_maps, core_ids=list(range(8)))
        xs = [np.asarray(res.results[b]["x_out"]) for b in range(8)]
    out = np.stack([xs[b].T for b in range(8)], axis=0)
    return np.ascontiguousarray(out.astype(np.float32))
